# version 06
# speedup vs baseline: 1.4334x; 1.0372x over previous
import numpy as np
import concourse.bass as bass
import concourse.mybir as mybir
from concourse.bass_utils import run_bass_kernel_spmd

F32 = mybir.dt.float32
BF16 = mybir.dt.bfloat16
I32 = mybir.dt.int32
AF = mybir.ActivationFunctionType
ALU = mybir.AluOpType
AX = mybir.AxisListType

NC = 8
B, L, E = 4, 4096, 1024
S = B * L
TPC = S // NC
R, NST, KCV = 64, 16, 4
NEXP, DFF, CAP = 16, 4096, 1280
NH, DH, KVR = 16, 64, 128
HD = NH * DH
EPC = E // NC
DUMP = 1 << 20


def _bf16():
    import ml_dtypes
    return ml_dtypes.bfloat16


class Prog:

    ENGS = ("sync", "scalar", "vector", "tensor", "gpsimd")

    def __init__(self):
        self.items = {e: [] for e in self.ENGS}
        self.cnt = {e: 0 for e in self.ENGS}

    def op(self, eng, f):
        pre = self.cnt[eng]
        def run(h, sems):
            if pre:
                h.wait_ge(sems[eng], pre)
            f(h)
            h.drain().then_inc(sems[eng], 1)
        self.items[eng].append(run)
        self.cnt[eng] = pre + 1
        return self.cnt[eng]

    def mm(self, f):
        return self.mm_group([f])

    def mm_group(self, fs):
        pre = self.cnt["tensor"]
        def run(h, sems):
            if pre:
                h.wait_ge(sems["tensor"], pre)
            insts = [f(h) for f in fs]
            insts[-1].then_inc(sems["tensor"], 1)
        self.items["tensor"].append(run)
        self.cnt["tensor"] = pre + 1
        return self.cnt["tensor"]

    def dma(self, eng, f):
        pre = self.cnt[eng]
        def run(h, sems):
            if pre:
                h.wait_ge(sems[eng], pre)
            f(h).then_inc(sems[eng], 16)
        self.items[eng].append(run)
        self.cnt[eng] = pre + 16
        return self.cnt[eng]

    def wait(self, eng, other, val):
        if val <= 0:
            return
        def run(h, sems):
            h.wait_ge(sems[other], val)
        self.items[eng].append(run)

    def fence(self, engines=None):
        engines = engines or self.ENGS
        snap = {e: self.cnt[e] for e in engines}
        for e in engines:
            for o in engines:
                if o != e:
                    self.wait(e, o, snap[o])

    def barrier_out(self, eng="gpsimd"):
        val = self.cnt[eng]
        def run(h, sems):
            h.wait_ge(sems[eng], val)
        self.items[eng].append(run)

    def build(self, nc):
        from contextlib import ExitStack
        with ExitStack() as st:
            sems = {e: st.enter_context(nc.semaphore("sem_" + e))
                    for e in self.ENGS}
            block = st.enter_context(nc.Block())

            def mk(eng_name):
                def section(h):
                    for it in self.items[eng_name]:
                        it(h, sems)
                return section
            block.sync(mk("sync"))
            block.scalar(mk("scalar"))
            block.vector(mk("vector"))
            block.tensor(mk("tensor"))
            block.gpsimd(mk("gpsimd"))


def ceil_div(a, b):
    return (a + b - 1) // b


def build_L1():
    nc = bass.Bass("TRN2", target_bir_lowering=False, debug=False)
    x_tok = nc.dram_tensor("x_tok", [TPC, E], F32, kind="ExternalInput").ap()
    wzT = nc.dram_tensor("wzT", [E, E], F32, kind="ExternalInput").ap()
    wpgT = nc.dram_tensor("wpgT", [E, 112], F32, kind="ExternalInput").ap()
    iden = nc.dram_tensor("iden", [128, 128], BF16, kind="ExternalInput").ap()

    NT = TPC // 128
    o_z = nc.dram_tensor("z", [E, TPC], F32, kind="ExternalOutput").ap()
    o_pg = nc.dram_tensor("pg", [112, TPC], F32, kind="ExternalOutput").ap()
    o_m = nc.dram_tensor("m", [128, NT], F32, kind="ExternalOutput").ap()
    o_r = nc.dram_tensor("r", [128, NT], F32, kind="ExternalOutput").ap()
    o_xn = nc.dram_tensor("xn", [TPC, E], F32, kind="ExternalOutput").ap()

    p = Prog()
    from contextlib import ExitStack
    with ExitStack() as st:
        En = st.enter_context
        sb_x = En(nc.sbuf_tensor("sb_x", [128, E], F32))
        sb_x2 = En(nc.sbuf_tensor("sb_x2", [128, E], F32))
        sb_xn = En(nc.sbuf_tensor("sb_xn", [128, E], BF16))
        sb_xnf = En(nc.sbuf_tensor("sb_xnf", [128, E], F32))
        sb_m = En(nc.sbuf_tensor("sb_m", [128, NT], F32))
        sb_r = En(nc.sbuf_tensor("sb_r", [128, NT], F32))
        sb_t1 = En(nc.sbuf_tensor("sb_t1", [128, 1], F32))
        sb_t2 = En(nc.sbuf_tensor("sb_t2", [128, 1], F32))
        sb_iden = En(nc.sbuf_tensor("sb_iden", [128, 128], BF16))
        sb_xnfm = En(nc.sbuf_tensor("sb_xnfm", [128, 8 * TPC], BF16))
        sb_w = En(nc.sbuf_tensor("sb_w", [128, 1024], F32))
        sb_wb = En(nc.sbuf_tensor("sb_wb", [128, 1024], BF16))
        sb_out = En(nc.sbuf_tensor("sb_out", [128, 512], F32))
        ps_tp = En(nc.psum_tensor("ps_tp", [128, 128], BF16))
        ps_o = En(nc.psum_tensor("ps_o", [128, 512], F32))

        ild = p.dma("sync", lambda h: h.dma_start(out=sb_iden[:], in_=iden))
        p.wait("tensor", "sync", ild)

        last_xn = 0
        last_sq = 0
        for t in range(NT):
            p.wait("sync", "vector", last_xn)
            p.wait("sync", "scalar", last_sq)
            ld = p.dma("sync", lambda h, t=t: h.dma_start(
                out=sb_x[:], in_=x_tok[t * 128:(t + 1) * 128, :]))
            p.wait("vector", "sync", ld)
            p.wait("scalar", "sync", ld)
            p.op("vector", lambda h, t=t: h.reduce_sum(sb_m[:, t:t + 1], sb_x[:], axis=AX.X))
            p.op("vector", lambda h, t=t: h.tensor_scalar_mul(sb_m[:, t:t + 1], sb_m[:, t:t + 1], 1.0 / E))
            last_sq = p.op("scalar", lambda h: h.activation(out=sb_x2[:], in_=sb_x[:], func=AF.Square))
            p.wait("vector", "scalar", last_sq)
            p.op("vector", lambda h: h.reduce_sum(sb_t1[:], sb_x2[:], axis=AX.X))
            p.op("vector", lambda h, t=t: h.tensor_single_scalar(
                out=sb_t2[:], in_=sb_m[:, t:t + 1], scalar=sb_m[:, t:t + 1], op=ALU.mult))
            p.op("vector", lambda h: h.tensor_scalar(
                out=sb_t1[:], in0=sb_t1[:], scalar1=1.0 / E, scalar2=sb_t2[:],
                op0=ALU.mult, op1=ALU.subtract))
            vs = p.op("vector", lambda h: h.tensor_scalar_add(sb_t1[:], sb_t1[:], 1e-5))
            p.wait("scalar", "vector", vs)
            p.op("scalar", lambda h: h.activation(out=sb_t2[:], in_=sb_t1[:], func=AF.Ln))
            sc = p.op("scalar", lambda h, t=t: h.activation(
                out=sb_r[:, t:t + 1], in_=sb_t2[:], func=AF.Exp, scale=-0.5))
            p.wait("vector", "scalar", sc)
            p.wait("vector", "gpsimd", p.cnt["gpsimd"])
            p.op("vector", lambda h, t=t: h.tensor_scalar(
                out=sb_xnf[:], in0=sb_x[:], scalar1=sb_m[:, t:t + 1],
                scalar2=sb_r[:, t:t + 1], op0=ALU.subtract, op1=ALU.mult))
            last_xn = p.op("vector", lambda h: h.tensor_copy(sb_xn[:], sb_xnf[:]))
            p.wait("gpsimd", "vector", last_xn)
            p.dma("gpsimd", lambda h, t=t: h.dma_start(
                out=o_xn[t * 128:(t + 1) * 128, :], in_=sb_xnf[:]))
            p.wait("tensor", "vector", last_xn)
            for d in range(8):
                p.wait("tensor", "vector", last_xn if d == 0 else p.cnt["vector"])
                p.mm(lambda h, d=d: h.transpose(
                    ps_tp[:], sb_xn[:, d * 128:(d + 1) * 128], sb_iden[:]))
                p.wait("vector", "tensor", p.cnt["tensor"])
                last_xn = p.op("vector", lambda h, d=d, t=t: h.tensor_copy(
                    sb_xnfm[:, d * TPC + t * 128: d * TPC + (t + 1) * 128], ps_tp[:]))

        vs = p.cnt["vector"]
        p.wait("gpsimd", "vector", vs)
        p.dma("gpsimd", lambda h: h.dma_start(out=o_m, in_=sb_m[:]))
        p.dma("gpsimd", lambda h: h.dma_start(out=o_r, in_=sb_r[:]))

        xn_ready = p.cnt["vector"]

        emst = {"cv": 0, "ev": 0}

        def emit_mm(wT_ap, FDIM, out_ap):
            for fc in range(ceil_div(FDIM, 128)):
                fw = min(128, FDIM - fc * 128)
                p.wait("sync", "vector", emst["cv"])
                wld = p.dma("sync", lambda h, fc=fc, fw=fw: h.dma_start(
                    out=sb_w[:, 0:8 * fw].rearrange("p (c f) -> p c f", c=8),
                    in_=wT_ap[:, fc * 128:fc * 128 + fw].rearrange("(c p) f -> p c f", p=128)))
                p.wait("vector", "sync", wld)
                emst["cv"] = p.op("vector", lambda h, fw=fw: h.tensor_copy(
                    sb_wb[:, 0:8 * fw], sb_w[:, 0:8 * fw]))
                for ntile in range(TPC // 512):
                    p.wait("tensor", "vector", max(emst["cv"], xn_ready, emst["ev"]))
                    fs = []
                    for d in range(8):
                        def mmf(h, d=d, fw=fw, ntile=ntile):
                            return h.matmul(
                                ps_o[0:fw, :],
                                sb_wb[:, d * fw:(d + 1) * fw],
                                sb_xnfm[:, d * TPC + ntile * 512: d * TPC + (ntile + 1) * 512],
                                start=(d == 0), stop=(d == 7))
                        fs.append(mmf)
                    mmdone = p.mm_group(fs)
                    p.wait("vector", "tensor", mmdone)
                    p.wait("vector", "gpsimd", p.cnt["gpsimd"])
                    emst["ev"] = p.op("vector", lambda h, fw=fw: h.tensor_copy(
                        sb_out[0:fw, 0:512], ps_o[0:fw, :]))
                    p.wait("gpsimd", "vector", emst["ev"])
                    p.dma("gpsimd", lambda h, fc=fc, fw=fw, ntile=ntile: h.dma_start(
                        out=out_ap[fc * 128:fc * 128 + fw, ntile * 512:(ntile + 1) * 512],
                        in_=sb_out[0:fw, 0:512]))

        emit_mm(wzT, E, o_z)
        emit_mm(wpgT, 112, o_pg)
        p.barrier_out("gpsimd")
        p.build(nc)
    return nc


def build_L2():
    nc = bass.Bass("TRN2", target_bir_lowering=False, debug=False)
    xT = nc.dram_tensor("xT", [EPC, S], F32, kind="ExternalInput").ap()
    dtU = nc.dram_tensor("dtU", [R, S], F32, kind="ExternalInput").ap()
    C_all = nc.dram_tensor("C_all", [NST, S], F32, kind="ExternalInput").ap()
    m_row = nc.dram_tensor("m_row", [1, S], F32, kind="ExternalInput").ap()
    r_row = nc.dram_tensor("r_row", [1, S], F32, kind="ExternalInput").ap()
    wdtT = nc.dram_tensor("wdtT", [R, EPC], F32, kind="ExternalInput").ap()
    bdt = nc.dram_tensor("bdt", [EPC, 1], F32, kind="ExternalInput").ap()
    alog = nc.dram_tensor("alog", [EPC, NST], F32, kind="ExternalInput").ap()
    dvec = nc.dram_tensor("dvec", [EPC, 1], F32, kind="ExternalInput").ap()
    cw = nc.dram_tensor("cw", [EPC, KCV], F32, kind="ExternalInput").ap()
    ones1 = nc.dram_tensor("ones1", [1, 128], F32, kind="ExternalInput").ap()

    o_y = nc.dram_tensor("y", [EPC, S], F32, kind="ExternalOutput").ap()

    p = Prog()
    from contextlib import ExitStack
    with ExitStack() as st:
        En = st.enter_context
        sb_dtU = En(nc.sbuf_tensor("sb_dtU", [R, L], F32))
        sb_wdt = En(nc.sbuf_tensor("sb_wdt", [R, EPC], F32))
        sb_bdt = En(nc.sbuf_tensor("sb_bdt", [EPC, 1], F32))
        sb_alog = En(nc.sbuf_tensor("sb_alog", [EPC, NST], F32))
        sb_A = En(nc.sbuf_tensor("sb_A", [EPC, NST], F32))
        sb_recip = En(nc.sbuf_tensor("sb_recip", [EPC, NST], F32))
        sb_small = En(nc.sbuf_tensor("sb_small", [EPC, NST], F32))
        sb_P = En(nc.sbuf_tensor("sb_P", [EPC, NST], F32))
        sb_negP = En(nc.sbuf_tensor("sb_negP", [EPC, NST], F32))
        sb_D = En(nc.sbuf_tensor("sb_D", [EPC, 1], F32))
        sb_cw = En(nc.sbuf_tensor("sb_cw", [EPC, KCV], F32))
        sb_ones1 = En(nc.sbuf_tensor("sb_ones1", [1, 128], F32))
        sb_mrow = En(nc.sbuf_tensor("sb_mrow", [1, 512], F32))
        sb_rrow = En(nc.sbuf_tensor("sb_rrow", [1, 512], F32))
        sb_crow = En(nc.sbuf_tensor("sb_crow", [1, L], F32))
        sb_xc = En(nc.sbuf_tensor("sb_xc", [128, 512], F32))
        sb_ts = En(nc.sbuf_tensor("sb_ts", [128, 512], F32))
        sb_xn = En(nc.sbuf_tensor("sb_xn", [128, L + 3], F32))
        sb_u = En(nc.sbuf_tensor("sb_u", [128, L], F32))
        sb_dt = En(nc.sbuf_tensor("sb_dt", [128, L], F32))
        sb_y = En(nc.sbuf_tensor("sb_y", [128, L], F32))
        sb_a = En(nc.sbuf_tensor("sb_a", [128, L + 1], F32))
        sb_s1 = En(nc.sbuf_tensor("sb_s1", [128, L + 1], F32))
        sb_s2 = En(nc.sbuf_tensor("sb_s2", [128, L], F32))
        sb_h = En(nc.sbuf_tensor("sb_h", [128, L], F32))
        sb_cb = En(nc.sbuf_tensor("sb_cb", [128, L], F32))
        sb_a2 = En(nc.sbuf_tensor("sb_a2", [128, L + 1], F32))
        ps_dt = En(nc.psum_tensor("ps_dt", [128, 512], F32))
        ps_m = En(nc.psum_tensor("ps_m", [128, 512], F32))
        ps_r = En(nc.psum_tensor("ps_r", [128, 512], F32))
        ps_c = En(nc.psum_tensor("ps_c", [128, 512], F32))

        for f in [lambda h: h.dma_start(out=sb_wdt[:], in_=wdtT),
                  lambda h: h.dma_start(out=sb_bdt[:], in_=bdt),
                  lambda h: h.dma_start(out=sb_alog[:], in_=alog),
                  lambda h: h.dma_start(out=sb_D[:], in_=dvec),
                  lambda h: h.dma_start(out=sb_cw[:], in_=cw),
                  lambda h: h.dma_start(out=sb_ones1[:], in_=ones1)]:
            p.dma("sync", f)
        prepld = p.cnt["sync"]
        p.wait("scalar", "sync", prepld)
        p.wait("vector", "sync", prepld)
        p.wait("tensor", "sync", prepld)

        p.op("scalar", lambda h: h.activation(out=sb_A[:], in_=sb_alog[:], func=AF.Exp))
        p.fence(("scalar", "vector"))
        p.op("vector", lambda h: h.tensor_scalar_mul(sb_A[:], sb_A[:], -1.0))
        p.op("vector", lambda h: h.tensor_scalar_add(sb_recip[:], sb_A[:], 1e-10))
        p.op("vector", lambda h: h.reciprocal(sb_recip[:], sb_recip[:]))
        p.fence(("scalar", "vector"))
        p.op("scalar", lambda h: h.activation(out=sb_small[:], in_=sb_A[:], func=AF.Abs))
        p.fence(("scalar", "vector"))
        p.op("vector", lambda h: h.tensor_single_scalar(
            out=sb_small[:], in_=sb_small[:], scalar=1e-5, op=ALU.is_lt))
        p.op("vector", lambda h: h.tensor_scalar(
            out=sb_P[:], in0=sb_small[:], scalar1=-1.0, scalar2=1.0,
            op0=ALU.mult, op1=ALU.add))
        p.op("vector", lambda h: h.tensor_tensor(
            out=sb_P[:], in0=sb_recip[:], in1=sb_P[:], op=ALU.mult))
        p.op("vector", lambda h: h.tensor_scalar_mul(sb_negP[:], sb_P[:], -1.0))
        p.fence()

        for b in range(B):
            base = b * L
            p.wait("sync", "tensor", p.cnt["tensor"])
            dld = p.dma("sync", lambda h, base=base: h.dma_start(
                out=sb_dtU[:], in_=dtU[:, base:base + L]))
            p.wait("tensor", "sync", dld)
            for ch in range(L // 512):
                p.fence(("tensor", "vector"))
                p.mm(lambda h, base=base, ch=ch: h.matmul(
                    ps_dt[:], sb_wdt[:], sb_dtU[:, ch * 512:(ch + 1) * 512],
                    start=True, stop=True))
                p.wait("vector", "tensor", p.cnt["tensor"])
                p.op("vector", lambda h, ch=ch: h.tensor_scalar(
                    out=sb_dt[:, ch * 512:(ch + 1) * 512], in0=ps_dt[:],
                    scalar1=sb_bdt[:], scalar2=None, op0=ALU.add, op1=ALU.bypass))
            p.fence(("scalar", "vector"))
            p.op("scalar", lambda h: h.activation(out=sb_a[:, 0:L], in_=sb_dt[:], func=AF.Abs))
            p.fence(("scalar",))
            p.op("scalar", lambda h: h.activation(out=sb_s1[:, 0:L], in_=sb_a[:, 0:L], func=AF.Exp, scale=-1.0))
            p.fence(("scalar",))
            p.op("scalar", lambda h: h.activation(out=sb_a[:, 0:L], in_=sb_s1[:, 0:L], func=AF.Ln, bias=1.0))
            p.fence(("scalar", "vector"))
            p.op("vector", lambda h: h.tensor_single_scalar(
                out=sb_s1[:, 0:L], in_=sb_dt[:], scalar=0.0, op=ALU.max))
            p.op("vector", lambda h: h.tensor_tensor(
                out=sb_dt[:], in0=sb_s1[:, 0:L], in1=sb_a[:, 0:L], op=ALU.add))
            p.op("vector", lambda h: h.memset(sb_xn[:, 0:3], 0.0))
            for ch in range(L // 512):
                p.dma("sync", lambda h, base=base, ch=ch: h.dma_start(
                    out=sb_mrow[:], in_=m_row[0:1, base + ch * 512: base + (ch + 1) * 512]))
                p.dma("sync", lambda h, base=base, ch=ch: h.dma_start(
                    out=sb_rrow[:], in_=r_row[0:1, base + ch * 512: base + (ch + 1) * 512]))
                xld = p.dma("sync", lambda h, base=base, ch=ch: h.dma_start(
                    out=sb_xc[:], in_=xT[:, base + ch * 512: base + (ch + 1) * 512]))
                p.wait("tensor", "sync", xld)
                p.fence(("tensor", "vector"))
                p.mm(lambda h: h.matmul(
                    ps_m[:], sb_ones1[:], sb_mrow[0:1, :],
                    start=True, stop=True))
                p.mm(lambda h: h.matmul(
                    ps_r[:], sb_ones1[:], sb_rrow[0:1, :],
                    start=True, stop=True))
                p.wait("vector", "tensor", p.cnt["tensor"])
                p.op("vector", lambda h: h.tensor_tensor(
                    out=sb_ts[:], in0=sb_xc[:], in1=ps_m[:], op=ALU.subtract))
                xnv = p.op("vector", lambda h, ch=ch: h.tensor_tensor(
                    out=sb_xn[:, 3 + ch * 512: 3 + (ch + 1) * 512], in0=sb_ts[:],
                    in1=ps_r[:], op=ALU.mult))
                p.wait("sync", "vector", p.cnt["vector"])
                p.wait("sync", "tensor", p.cnt["tensor"])
            p.fence(("scalar", "vector"))
            p.op("vector", lambda h: h.tensor_single_scalar(
                out=sb_s1[:, 0:L], in_=sb_xn[:, 0:L], scalar=sb_cw[:, 0:1], op=ALU.mult))
            p.op("vector", lambda h: h.scalar_tensor_tensor(
                out=sb_s2[:], in0=sb_xn[:, 1:1 + L], scalar=sb_cw[:, 1:2], in1=sb_s1[:, 0:L],
                op0=ALU.mult, op1=ALU.add))
            p.op("vector", lambda h: h.scalar_tensor_tensor(
                out=sb_s1[:, 0:L], in0=sb_xn[:, 2:2 + L], scalar=sb_cw[:, 2:3], in1=sb_s2[:],
                op0=ALU.mult, op1=ALU.add))
            cvv = p.op("vector", lambda h: h.scalar_tensor_tensor(
                out=sb_s2[:], in0=sb_xn[:, 3:3 + L], scalar=sb_cw[:, 3:4], in1=sb_s1[:, 0:L],
                op0=ALU.mult, op1=ALU.add))
            p.wait("scalar", "vector", cvv)
            su = p.op("scalar", lambda h: h.activation(out=sb_u[:], in_=sb_s2[:], func=AF.Silu))
            p.wait("vector", "scalar", su)
            p.op("vector", lambda h: h.memset(sb_y[:], 0.0))
            p.fence()
            p.fence(("scalar", "vector"))
            p.op("vector", lambda h: h.memset(sb_a[:, 0:1], 0.0))
            p.op("vector", lambda h: h.memset(sb_a2[:, 0:1], 0.0))
            scan_done = {0: 0, 1: 0}
            prodv = 0
            for n in range(NST):
                par = n % 2
                a_t = sb_a if par == 0 else sb_a2
                p.wait("scalar", "vector", scan_done[par])
                p.op("scalar", lambda h, n=n, a_t=a_t: h.activation(
                    out=a_t[:, 1:1 + L], in_=sb_dt[:], func=AF.Exp, scale=sb_A[:, n:n + 1]))
                expv = p.cnt["scalar"]
                p.wait("vector", "scalar", expv)
                p.op("vector", lambda h, n=n, a_t=a_t: h.tensor_scalar(
                    out=sb_s1[:, 1:1 + L], in0=a_t[:, 1:1 + L], scalar1=sb_P[:, n:n + 1],
                    scalar2=sb_negP[:, n:n + 1], op0=ALU.mult, op1=ALU.add))
                p.op("vector", lambda h, n=n: h.scalar_tensor_tensor(
                    out=sb_s2[:], in0=sb_dt[:], scalar=sb_small[:, n:n + 1], in1=sb_s1[:, 1:1 + L],
                    op0=ALU.mult, op1=ALU.add))
                p.op("vector", lambda h: h.tensor_tensor(
                    out=sb_s1[:, 1:1 + L], in0=sb_s2[:], in1=sb_u[:], op=ALU.mult))
                p.op("vector", lambda h: h.tensor_copy(sb_s1[:, 0:1], sb_s1[:, 1:2]))
                scan_done[par] = p.op("vector", lambda h, a_t=a_t: h.tensor_tensor_scan(
                    out=sb_h[:], data0=a_t[:, 0:L], data1=sb_s1[:, 0:L], initial=0.0,
                    op0=ALU.mult, op1=ALU.add))
                p.wait("sync", "scalar", p.cnt["scalar"])
                cld = p.dma("sync", lambda h, base=base, n=n: h.dma_start(
                    out=sb_crow[:], in_=C_all[n:n + 1, base:base + L]))
                p.wait("tensor", "sync", cld)
                p.wait("tensor", "scalar", p.cnt["scalar"])
                p.wait("scalar", "vector", prodv)
                for ch in range(L // 512):
                    p.mm(lambda h, ch=ch: h.matmul(
                        ps_c[:], sb_ones1[:], sb_crow[0:1, ch * 512:(ch + 1) * 512],
                        start=True, stop=True))
                    mmv = p.cnt["tensor"]
                    p.wait("scalar", "tensor", mmv)
                    sv = p.op("scalar", lambda h, ch=ch: h.activation(
                        out=sb_cb[:, ch * 512:(ch + 1) * 512], in_=ps_c[:], func=AF.Copy))
                    p.wait("tensor", "scalar", sv)
                cbv = p.cnt["scalar"]
                p.wait("vector", "scalar", cbv)
                p.op("vector", lambda h: h.tensor_tensor(
                    out=sb_s2[:], in0=sb_h[:], in1=sb_cb[:], op=ALU.mult))
                prodv = p.op("vector", lambda h: h.tensor_tensor(
                    out=sb_y[:], in0=sb_y[:], in1=sb_s2[:], op=ALU.add))
            yv = p.op("vector", lambda h: h.scalar_tensor_tensor(
                out=sb_s2[:], in0=sb_u[:], scalar=sb_D[:], in1=sb_y[:],
                op0=ALU.mult, op1=ALU.add))
            p.wait("gpsimd", "vector", yv)
            p.dma("gpsimd", lambda h, base=base: h.dma_start(
                out=o_y[:, base:base + L], in_=sb_s2[:]))
            p.fence()
        p.barrier_out("gpsimd")
        p.build(nc)
    return nc


def host_L2(inputs, cache, run):
    x = np.asarray(inputs["L0_attn_x"], np.float32).reshape(S, E)
    xT = np.ascontiguousarray(x.T)
    pg = cache["pg_all"]
    dtU = np.ascontiguousarray(pg[0:R])
    C_all = np.ascontiguousarray(pg[R + NST:R + 2 * NST])
    m_row = cache["m_all"].reshape(1, S)
    r_row = cache["r_all"].reshape(1, S)
    wdtT = np.asarray(inputs["L0_attn_W_dt"], np.float32).T
    alog = np.asarray(inputs["L0_attn_A_log"], np.float32)
    bdt = np.asarray(inputs["L0_attn_b_dt"], np.float32).reshape(E, 1)
    dvec = np.asarray(inputs["L0_attn_D"], np.float32).reshape(E, 1)
    cw = np.asarray(inputs["L0_attn_conv_w"], np.float32)[:, 0, :]
    ones1 = np.ones((1, 128), np.float32)
    nc = build_L2()
    in_maps = []
    for c in range(NC):
        sl = slice(c * EPC, (c + 1) * EPC)
        in_maps.append(dict(
            xT=np.ascontiguousarray(xT[sl]), dtU=dtU, C_all=C_all,
            m_row=m_row, r_row=r_row,
            wdtT=np.ascontiguousarray(wdtT[:, sl]),
            bdt=np.ascontiguousarray(bdt[sl]), alog=np.ascontiguousarray(alog[sl]),
            dvec=np.ascontiguousarray(dvec[sl]), cw=np.ascontiguousarray(cw[sl]),
            ones1=ones1))
    res = run(nc, in_maps)
    cache["yssm_all"] = np.concatenate([np.asarray(res[c]["y"]) for c in range(NC)], 0)
    pass


def build_L3():
    nc = bass.Bass("TRN2", target_bir_lowering=False, debug=False)
    gate = nc.dram_tensor("gate", [S, NEXP], F32, kind="ExternalInput").ap()
    triu = nc.dram_tensor("triu", [128, 128], BF16, kind="ExternalInput").ap()
    ones128 = nc.dram_tensor("ones128", [128, 128], BF16, kind="ExternalInput").ap()
    iota16 = nc.dram_tensor("iota16", [128, NEXP], F32, kind="ExternalInput").ap()
    ones16 = nc.dram_tensor("ones16", [128, NEXP], F32, kind="ExternalInput").ap()

    NT = S // 128
    o_key = nc.dram_tensor("key", [128, NT], F32, kind="ExternalOutput").ap()
    o_prob = nc.dram_tensor("prob", [128, NT], F32, kind="ExternalOutput").ap()
    o_esel = nc.dram_tensor("esel", [128, NT], F32, kind="ExternalOutput").ap()
    o_rank = nc.dram_tensor("rank", [128, NT], F32, kind="ExternalOutput").ap()

    p = Prog()
    from contextlib import ExitStack
    with ExitStack() as st:
        En = st.enter_context
        sb_g = En(nc.sbuf_tensor("sb_g", [128, NEXP], F32))
        sb_gall = En(nc.sbuf_tensor("sb_gall", [128, NEXP * NT], F32))
        sb_e = En(nc.sbuf_tensor("sb_e", [128, NEXP], F32))
        sb_pr = En(nc.sbuf_tensor("sb_pr", [128, NEXP * NT], F32))
        sb_mk = En(nc.sbuf_tensor("sb_mk", [128, NEXP * NT], F32))
        sb_mkb = En(nc.sbuf_tensor("sb_mkb", [128, NEXP * NT], BF16))
        sb_cs = En(nc.sbuf_tensor("sb_cs", [128, NEXP * NT], F32))
        sb_off = En(nc.sbuf_tensor("sb_off", [128, NEXP * NT], F32))
        sb_mx = En(nc.sbuf_tensor("sb_mx", [128, 1], F32))
        sb_nmx = En(nc.sbuf_tensor("sb_nmx", [128, 1], F32))
        sb_se = En(nc.sbuf_tensor("sb_se", [128, 1], F32))
        sb_rec = En(nc.sbuf_tensor("sb_rec", [128, 1], F32))
        sb_c16 = En(nc.sbuf_tensor("sb_c16", [128, NEXP], F32))
        sb_t16 = En(nc.sbuf_tensor("sb_t16", [128, NEXP], F32))
        sb_t16b = En(nc.sbuf_tensor("sb_t16b", [128, NEXP], F32))
        sb_zero16 = En(nc.sbuf_tensor("sb_zero16", [128, NEXP], F32))
        sb_key = En(nc.sbuf_tensor("sb_key", [128, NT], F32))
        sb_prob = En(nc.sbuf_tensor("sb_prob", [128, NT], F32))
        sb_esel = En(nc.sbuf_tensor("sb_esel", [128, NT], F32))
        sb_rank = En(nc.sbuf_tensor("sb_rank", [128, NT], F32))
        sb_rout = En(nc.sbuf_tensor("sb_rout", [128, NT], F32))
        sb_triu = En(nc.sbuf_tensor("sb_triu", [128, 128], BF16))
        sb_o128 = En(nc.sbuf_tensor("sb_o128", [128, 128], BF16))
        sb_i16 = En(nc.sbuf_tensor("sb_i16", [128, NEXP], F32))
        sb_1s16 = En(nc.sbuf_tensor("sb_1s16", [128, NEXP], F32))
        ps_a = En(nc.psum_tensor("ps_a", [128, NEXP], F32))
        ps_b = En(nc.psum_tensor("ps_b", [128, NEXP], F32))

        for f in [lambda h: h.dma_start(out=sb_triu[:], in_=triu),
                  lambda h: h.dma_start(out=sb_o128[:], in_=ones128),
                  lambda h: h.dma_start(out=sb_i16[:], in_=iota16),
                  lambda h: h.dma_start(out=sb_1s16[:], in_=ones16)]:
            p.dma("sync", f)
        prepld = p.cnt["sync"]
        for e in ("scalar", "vector", "tensor"):
            p.wait(e, "sync", prepld)
        p.op("vector", lambda h: h.memset(sb_zero16[:], 0.0))
        gld = p.dma("sync", lambda h: h.dma_start(
            out=sb_gall[:].rearrange("p (t e) -> p t e", e=NEXP),
            in_=gate.rearrange("(t p) e -> p t e", p=128)))
        p.wait("vector", "sync", gld)
        p.wait("scalar", "sync", gld)
        gall3 = sb_gall.rearrange("p (t e) -> p t e", e=NEXP)

        mk3 = sb_mk.rearrange("p (t e) -> p t e", e=NEXP)
        mkb3 = sb_mkb.rearrange("p (t e) -> p t e", e=NEXP)
        pr3 = sb_pr.rearrange("p (t e) -> p t e", e=NEXP)
        cs3 = sb_cs.rearrange("p (t e) -> p t e", e=NEXP)
        off3 = sb_off.rearrange("p (t e) -> p t e", e=NEXP)

        for t in range(NT):
            p.op("vector", lambda h, t=t: h.reduce_max(sb_mx[:], gall3[:, t, :], axis=AX.X))
            vm = p.op("vector", lambda h: h.tensor_scalar_mul(sb_nmx[:], sb_mx[:], -1.0))
            p.wait("scalar", "vector", vm)
            sc = p.op("scalar", lambda h, t=t: h.activation(
                out=sb_e[:], in_=gall3[:, t, :], func=AF.Exp, bias=sb_nmx[:],
                accum_out=sb_se[:]))
            p.wait("vector", "scalar", sc)
            p.op("vector", lambda h: h.reciprocal(sb_rec[:], sb_se[:]))
            p.op("vector", lambda h, t=t: h.tensor_single_scalar(
                out=pr3[:, t, :], in_=sb_e[:], scalar=sb_rec[:], op=ALU.mult))
            p.op("vector", lambda h, t=t: h.tensor_single_scalar(
                out=sb_t16[:], in_=gall3[:, t, :], scalar=sb_mx[:], op=ALU.is_equal))
            p.op("vector", lambda h: h.tensor_tensor_scan(
                out=sb_c16[:], data0=sb_1s16[:], data1=sb_t16[:], initial=0.0,
                op0=ALU.mult, op1=ALU.add))
            p.op("vector", lambda h: h.tensor_single_scalar(
                out=sb_t16b[:], in_=sb_c16[:], scalar=1.0, op=ALU.is_equal))
            p.op("vector", lambda h, t=t: h.tensor_tensor(
                out=mk3[:, t, :], in0=sb_t16[:], in1=sb_t16b[:], op=ALU.mult))
            p.op("vector", lambda h, t=t: h.tensor_copy(mkb3[:, t, :], mk3[:, t, :]))

        for t in range(NT):
            p.wait("tensor", "vector", p.cnt["vector"])
            mmv = p.mm(lambda h, t=t: h.matmul(
                ps_a[:], sb_o128[:], mkb3[:, t, :], start=True, stop=True))
            p.wait("vector", "tensor", mmv)
            ev = p.op("vector", lambda h, t=t: h.tensor_copy(cs3[:, t, :], ps_a[:]))
            p.wait("tensor", "vector", ev)

        for e in range(NEXP):
            p.op("vector", lambda h, e=e: h.tensor_tensor_scan(
                out=off3[:, :, e], data0=sb_1s16[:, 0:NT] if NT <= NEXP else sb_o128[:, 0:NT],
                data1=cs3[:, :, e], initial=0.0, op0=ALU.mult, op1=ALU.add))

        for t in range(NT):
            p.wait("tensor", "vector", p.cnt["vector"])
            mmv = p.mm(lambda h, t=t: h.matmul(
                ps_b[:], sb_triu[:], mkb3[:, t, :], start=True, stop=True))
            p.wait("vector", "tensor", mmv)
            if t == 0:
                p.op("vector", lambda h: h.tensor_tensor(
                    out=sb_c16[:], in0=ps_b[:], in1=sb_zero16[:], op=ALU.add))
            else:
                p.op("vector", lambda h, t=t: h.tensor_tensor(
                    out=sb_c16[:], in0=ps_b[:], in1=off3[:, t - 1, :], op=ALU.add))
            ev = p.cnt["vector"]
            p.wait("tensor", "vector", ev)
            p.op("vector", lambda h: h.tensor_single_scalar(
                out=sb_t16[:], in_=sb_c16[:], scalar=float(CAP), op=ALU.is_le))
            p.op("vector", lambda h, t=t: h.tensor_tensor(
                out=sb_t16[:], in0=sb_t16[:], in1=mk3[:, t, :], op=ALU.mult))
            p.op("vector", lambda h, t=t: h.scalar_tensor_tensor(
                out=sb_t16b[:], in0=sb_c16[:], scalar=-1.0, in1=sb_t16[:],
                op0=ALU.add, op1=ALU.mult, accum_out=sb_rank[:, t:t + 1]))
            p.op("vector", lambda h, t=t: h.scalar_tensor_tensor(
                out=sb_t16b[:], in0=sb_i16[:], scalar=1.0, in1=sb_t16[:],
                op0=ALU.mult, op1=ALU.mult, accum_out=sb_esel[:, t:t + 1]))
            p.op("vector", lambda h, t=t: h.scalar_tensor_tensor(
                out=sb_t16b[:], in0=pr3[:, t, :], scalar=1.0, in1=sb_t16[:],
                op0=ALU.mult, op1=ALU.mult, accum_out=sb_prob[:, t:t + 1]))
            p.op("vector", lambda h, t=t: h.tensor_scalar(
                out=sb_t16b[:], in0=sb_t16[:], scalar1=1.0, scalar2=0.0,
                op0=ALU.mult, op1=ALU.add, accum_out=sb_rout[:, t:t + 1]))
            p.op("vector", lambda h, t=t: h.tensor_scalar(
                out=sb_key[:, t:t + 1], in0=sb_esel[:, t:t + 1], scalar1=float(CAP),
                scalar2=sb_rank[:, t:t + 1], op0=ALU.mult, op1=ALU.add))
            p.op("vector", lambda h, t=t: h.tensor_scalar(
                out=sb_t16b[:, 0:1], in0=sb_rout[:, t:t + 1], scalar1=-float(DUMP),
                scalar2=float(DUMP), op0=ALU.mult, op1=ALU.add))
            p.op("vector", lambda h, t=t: h.tensor_tensor(
                out=sb_key[:, t:t + 1], in0=sb_key[:, t:t + 1], in1=sb_t16b[:, 0:1],
                op=ALU.add))

        last = p.cnt["vector"]
        p.wait("gpsimd", "vector", last)
        p.dma("gpsimd", lambda h: h.dma_start(out=o_key, in_=sb_key[:]))
        p.dma("gpsimd", lambda h: h.dma_start(out=o_prob, in_=sb_prob[:]))
        p.dma("gpsimd", lambda h: h.dma_start(out=o_esel, in_=sb_esel[:]))
        p.dma("gpsimd", lambda h: h.dma_start(out=o_rank, in_=sb_rank[:]))
        p.barrier_out("gpsimd")
        p.build(nc)
    return nc


def host_L3(inputs, cache, run):
    gate_tok = np.ascontiguousarray(cache["pg_all"][R + 2 * NST:R + 2 * NST + NEXP].T)
    triu = np.triu(np.ones((128, 128), np.float32)).astype(_bf16())
    ones128 = np.ones((128, 128), np.float32).astype(_bf16())
    iota16 = np.tile(np.arange(NEXP, dtype=np.float32), (128, 1))
    ones16 = np.ones((128, NEXP), np.float32)
    nc = build_L3()
    in_maps = [dict(gate=gate_tok, triu=triu, ones128=ones128,
                    iota16=iota16, ones16=ones16) for _ in range(NC)]
    res = run(nc, in_maps)
    r0 = res[0]
    for nm in ("key", "prob", "esel", "rank"):
        cache[nm + "_pt"] = np.asarray(r0[nm])
        cache[nm + "_flat"] = np.asarray(r0[nm]).T.reshape(-1)


def build_L4():
    nc = bass.Bass("TRN2", target_bir_lowering=False, debug=False)
    xn_full = nc.dram_tensor("xn_full", [S, E], F32, kind="ExternalInput").ap()
    key_pt = nc.dram_tensor("key_pt", [128, S // 128], F32, kind="ExternalInput").ap()
    iotaN = nc.dram_tensor("iotaN", [128, 2 * CAP], F32, kind="ExternalInput").ap()
    lhs3 = nc.dram_tensor("lhs3", [128, 3 * (S // 128)], BF16, kind="ExternalInput").ap()
    coef3 = nc.dram_tensor("coef3", [3, 1], BF16, kind="ExternalInput").ap()
    base_col = nc.dram_tensor("base_col", [128, 1], F32, kind="ExternalInput").ap()
    w1 = nc.dram_tensor("w1", [2, E, DFF], F32, kind="ExternalInput").ap()
    w2 = nc.dram_tensor("w2", [2, DFF, E], F32, kind="ExternalInput").ap()
    idenb = nc.dram_tensor("idenb", [128, 128], BF16, kind="ExternalInput").ap()
    idenf = nc.dram_tensor("idenf", [128, 128], F32, kind="ExternalInput").ap()

    NSL = 2 * CAP
    o_eo = nc.dram_tensor("eo", [NSL, E], F32, kind="ExternalOutput").ap()

    p = Prog()
    from contextlib import ExitStack
    with ExitStack() as st:
        En = st.enter_context
        NT = S // 128
        sb_keys = En(nc.sbuf_tensor("sb_keys", [128, NT], F32))
        sb_base = En(nc.sbuf_tensor("sb_base", [128, 1], F32))
        sb_t = En(nc.sbuf_tensor("sb_t", [128, NT], F32))
        sb_t2 = En(nc.sbuf_tensor("sb_t2", [128, NT], F32))
        sb_ti = En(nc.sbuf_tensor("sb_ti", [128, NSL // 128], I32))
        sb_tif = En(nc.sbuf_tensor("sb_tif", [128, NSL // 128], F32))
        sb_ib = En(nc.sbuf_tensor("sb_ib", [128, 128], BF16))
        sb_if = En(nc.sbuf_tensor("sb_if", [128, 128], F32))
        sb_stage = En(nc.sbuf_tensor("sb_stage", [128, 1024], F32))
        sb_dispb = En(nc.sbuf_tensor("sb_dispb", [128, 8 * NSL], BF16))
        sb_db = En(nc.sbuf_tensor("sb_db", [128, E], BF16))
        sb_w1b = En(nc.sbuf_tensor("sb_w1b", [128, 8 * DFF], BF16))
        sb_hx = En(nc.sbuf_tensor("sb_hx", [128, 2 * 256], F32))
        sb_v1 = En(nc.sbuf_tensor("sb_v1", [128, 2 * 256], F32))
        sb_v2 = En(nc.sbuf_tensor("sb_v2", [128, 2 * 256], F32))
        sb_hst = En(nc.sbuf_tensor("sb_hst", [128, 32 * 256], BF16))
        sb_eo = En(nc.sbuf_tensor("sb_eo", [128, 8 * 256], F32))
        sb_eot = En(nc.sbuf_tensor("sb_eot", [128, 2 * 128], F32))
        ps_mm = En(nc.psum_tensor("ps_mm", [128, 512], F32))
        ps_tb = En(nc.psum_tensor("ps_tb", [128, 128], BF16))
        ps_tf = En(nc.psum_tensor("ps_tf", [128, 128], F32))
        inner = ExitStack()
        En2 = inner.enter_context
        En2x = inner.enter_context
        sb_iota = En2(nc.sbuf_tensor("sb_iota", [128, NSL], F32))
        sb_l3 = En2(nc.sbuf_tensor("sb_l3", [128, 3 * NT], BF16))
        sb_me = En2(nc.sbuf_tensor("sb_me", [128, NSL], BF16))
        ps_i5 = [En2(nc.psum_tensor(f"ps_i{i}", [3, 512], F32)) for i in range(5)]
        sb_c3 = En2(nc.sbuf_tensor("sb_c3", [3, NSL], F32))
        sb_coef = En2(nc.sbuf_tensor("sb_coef", [3, 1], BF16))
        sb_c3b = En2(nc.sbuf_tensor("sb_c3b", [3, NSL], BF16))
        sb_row = En2(nc.sbuf_tensor("sb_row", [1, NSL], F32))

        for f in [lambda h: h.dma_start(out=sb_keys[:], in_=key_pt),
                  lambda h: h.dma_start(out=sb_base[:], in_=base_col),
                  lambda h: h.dma_start(out=sb_iota[:], in_=iotaN),
                  lambda h: h.dma_start(out=sb_l3[:], in_=lhs3),
                  lambda h: h.dma_start(out=sb_coef[:], in_=coef3),
                  lambda h: h.dma_start(out=sb_ib[:], in_=idenb),
                  lambda h: h.dma_start(out=sb_if[:], in_=idenf)]:
            p.dma("sync", f)
        prepld = p.cnt["sync"]
        for e in ("scalar", "vector", "tensor"):
            p.wait(e, "sync", prepld)

        p.op("vector", lambda h: h.tensor_single_scalar(
            out=sb_keys[:], in_=sb_keys[:], scalar=sb_base[:], op=ALU.subtract))
        p.op("vector", lambda h: h.tensor_single_scalar(
            out=sb_t[:], in_=sb_keys[:], scalar=0.0, op=ALU.is_ge))
        p.op("vector", lambda h: h.tensor_single_scalar(
            out=sb_t2[:], in_=sb_keys[:], scalar=float(2 * CAP), op=ALU.is_lt))
        p.op("vector", lambda h: h.tensor_tensor(
            out=sb_t[:], in0=sb_t[:], in1=sb_t2[:], op=ALU.mult))
        p.op("vector", lambda h: h.tensor_scalar_add(sb_keys[:], sb_keys[:], -float(DUMP)))
        p.op("vector", lambda h: h.tensor_tensor(
            out=sb_keys[:], in0=sb_keys[:], in1=sb_t[:], op=ALU.mult))
        kc = p.op("vector", lambda h: h.tensor_scalar_add(sb_keys[:], sb_keys[:], float(DUMP)))

        p.wait("tensor", "vector", kc)
        NCH = NSL // 512
        for t in range(NT):
            p.wait("vector", "tensor", p.cnt["tensor"])
            mev = p.op("vector", lambda h, t=t: h.tensor_single_scalar(
                out=sb_me[:], in_=sb_iota[:], scalar=sb_keys[:, t:t + 1], op=ALU.is_equal))
            p.wait("tensor", "vector", mev)
            fs = []
            for c in range(NCH):
                def mmf(h, t=t, c=c):
                    return h.matmul(
                        ps_i5[c][:], sb_l3[:, 3 * t:3 * t + 3],
                        sb_me[:, c * 512:(c + 1) * 512],
                        start=(t == 0), stop=(t == NT - 1), skip_group_check=True)
                fs.append(mmf)
            p.mm_group(fs)
        mmv = p.cnt["tensor"]
        p.wait("vector", "tensor", mmv)
        for c in range(NCH):
            ev = p.op("vector", lambda h, c=c: h.tensor_copy(
                sb_c3[:, c * 512:(c + 1) * 512], ps_i5[c][:]))
        p.wait("tensor", "vector", p.cnt["vector"])
        cfv = p.op("vector", lambda h: h.tensor_copy(sb_c3b[:], sb_c3[:]))
        p.wait("tensor", "vector", cfv)
        for c in range(NCH):
            mmv = p.mm(lambda h, c=c: h.matmul(
                ps_mm[0:1, :], sb_coef[:], sb_c3b[:, c * 512:(c + 1) * 512],
                start=True, stop=True))
            p.wait("vector", "tensor", mmv)
            ev = p.op("vector", lambda h, c=c: h.tensor_copy(
                sb_row[:, c * 512:(c + 1) * 512], ps_mm[0:1, :]))
            p.wait("tensor", "vector", ev)
        for i in range(NSL // 128):
            p.wait("tensor", "vector", p.cnt["vector"])
            mmv = p.mm(lambda h, i=i: h.transpose(
                ps_tf[0:128, 0:1], sb_row[:, i * 128:(i + 1) * 128], sb_if[0:1, 0:1]))
            p.wait("vector", "tensor", mmv)
            p.op("vector", lambda h, i=i: h.tensor_copy(sb_tif[:, i:i + 1], ps_tf[0:128, 0:1]))
        tv = p.op("vector", lambda h: h.tensor_copy(sb_ti[:], sb_tif[:]))

        p.wait("gpsimd", "vector", tv)
        for i in range(NSL // 128):
            gld = p.dma("gpsimd", lambda h, i=i: h.indirect_dma_start(
                out=sb_stage[:, 0:E], out_offset=None, in_=xn_full,
                in_offset=bass.IndirectOffsetOnAxis(ap=sb_ti[:, i:i + 1], axis=0),
                oob_is_err=False, bounds_check=S))
            p.wait("vector", "gpsimd", gld)
            cv = p.op("vector", lambda h: h.tensor_copy(sb_db[:], sb_stage[:, 0:E]))
            p.wait("tensor", "vector", cv)
            for d in range(8):
                p.wait("tensor", "vector", p.cnt["vector"])
                mmv = p.mm(lambda h, d=d: h.transpose(
                    ps_tb[:], sb_db[:, d * 128:(d + 1) * 128], sb_ib[:]))
                p.wait("vector", "tensor", mmv)
                p.op("vector", lambda h, d=d, i=i: h.tensor_copy(
                    sb_dispb[:, d * NSL + i * 128: d * NSL + (i + 1) * 128], ps_tb[:]))
            p.wait("gpsimd", "vector", p.cnt["vector"])

        disp_ready = p.cnt["vector"]
        inner.close()
        sb_w2b = En(nc.sbuf_tensor("sb_w2b", [128, 32 * E], BF16))
        ps_mm2 = En(nc.psum_tensor("ps_mm2", [128, 512], F32))

        for e in range(2):
            sb_stage2 = sb_eo[:, 0:1024]
            p.wait("sync", "gpsimd", p.cnt["gpsimd"])
            cvd = [0, 0]
            for ii, (dc, hh) in enumerate([(d, q) for d in range(8) for q in range(4)]):
                par = ii % 2
                stg = sb_stage if par == 0 else sb_stage2
                p.wait("sync", "vector", cvd[par])
                ld = p.dma("sync", lambda h, e=e, dc=dc, hh=hh, stg=stg: h.dma_start(
                    out=stg[:], in_=w1[e, dc * 128:(dc + 1) * 128,
                                       hh * 1024:(hh + 1) * 1024]))
                p.wait("vector", "sync", ld)
                cvd[par] = p.op("vector", lambda h, dc=dc, hh=hh, stg=stg: h.tensor_copy(
                    sb_w1b[:, dc * DFF + hh * 1024: dc * DFF + (hh + 1) * 1024],
                    stg[:]))
            cvd = [0, 0]
            for fc in range(32):
                par = fc % 2
                stg = sb_stage if par == 0 else sb_stage2
                p.wait("sync", "vector", cvd[par])
                ld = p.dma("sync", lambda h, e=e, fc=fc, stg=stg: h.dma_start(
                    out=stg[:, 0:E], in_=w2[e, fc * 128:(fc + 1) * 128, :]))
                p.wait("vector", "sync", ld)
                cvd[par] = p.op("vector", lambda h, fc=fc, stg=stg: h.tensor_copy(
                    sb_w2b[:, fc * E:(fc + 1) * E], stg[:, 0:E]))
            wready = p.cnt["vector"]
            p.wait("tensor", "vector", max(wready, disp_ready))

            for sti, (s0, sw) in enumerate([(k * 256, 256) for k in range(5)]):
                slot0 = e * CAP + s0
                cpv = [0, 0]
                for fc in range(32):
                    par = fc % 2
                    pmm = ps_mm if par == 0 else ps_mm2
                    o1, o2 = par * 256, par * 256 + 256
                    p.wait("tensor", "vector", cpv[par])
                    fs = []
                    for dc in range(8):
                        def mmf(h, dc=dc, fc=fc, slot0=slot0, sw=sw, pmm=pmm):
                            return h.matmul(
                                pmm[:, 0:sw],
                                sb_w1b[:, dc * DFF + fc * 128: dc * DFF + (fc + 1) * 128],
                                sb_dispb[:, dc * NSL + slot0: dc * NSL + slot0 + sw],
                                start=(dc == 0), stop=(dc == 7))
                        fs.append(mmf)
                    mmv = p.mm_group(fs)
                    p.wait("vector", "tensor", mmv)
                    cpv[par] = p.op("vector", lambda h, sw=sw, pmm=pmm, o1=o1: h.tensor_copy(
                        sb_hx[:, o1:o1 + sw], pmm[:, 0:sw]))
                    p.op("vector", lambda h, sw=sw, o1=o1: h.tensor_tensor(
                        out=sb_v1[:, o1:o1 + sw], in0=sb_hx[:, o1:o1 + sw],
                        in1=sb_hx[:, o1:o1 + sw], op=ALU.mult))
                    p.op("vector", lambda h, sw=sw, o1=o1: h.tensor_scalar(
                        out=sb_v1[:, o1:o1 + sw], in0=sb_v1[:, o1:o1 + sw],
                        scalar1=0.044715, scalar2=1.0, op0=ALU.mult, op1=ALU.add))
                    vv = p.op("vector", lambda h, sw=sw, o1=o1: h.tensor_tensor(
                        out=sb_v2[:, o1:o1 + sw], in0=sb_v1[:, o1:o1 + sw],
                        in1=sb_hx[:, o1:o1 + sw], op=ALU.mult))
                    p.wait("scalar", "vector", vv)
                    sv = p.op("scalar", lambda h, sw=sw, o1=o1: h.activation(
                        out=sb_v2[:, o1:o1 + sw], in_=sb_v2[:, o1:o1 + sw], func=AF.Tanh,
                        scale=0.7978845608028654))
                    p.wait("vector", "scalar", sv)
                    p.op("vector", lambda h, sw=sw, o1=o1: h.tensor_scalar(
                        out=sb_v2[:, o1:o1 + sw], in0=sb_v2[:, o1:o1 + sw], scalar1=1.0,
                        scalar2=0.5, op0=ALU.add, op1=ALU.mult))
                    p.op("vector", lambda h, fc=fc, sw=sw, o1=o1: h.tensor_tensor(
                        out=sb_hst[:, fc * 256: fc * 256 + sw], in0=sb_v2[:, o1:o1 + sw],
                        in1=sb_hx[:, o1:o1 + sw], op=ALU.mult))
                hready = p.cnt["vector"]
                p.wait("tensor", "vector", hready)
                for dc in range(8):
                    fs = []
                    for fc in range(32):
                        def mmf(h, dc=dc, fc=fc, sw=sw):
                            return h.matmul(
                                ps_mm[:, 0:sw],
                                sb_w2b[:, fc * E + dc * 128: fc * E + (dc + 1) * 128],
                                sb_hst[:, fc * 256: fc * 256 + sw],
                                start=(fc == 0), stop=(fc == 31))
                        fs.append(mmf)
                    mmv = p.mm_group(fs)
                    p.wait("vector", "tensor", mmv)
                    ev = p.op("vector", lambda h, dc=dc, sw=sw: h.tensor_copy(
                        sb_eo[:, dc * 256: dc * 256 + sw], ps_mm[:, 0:sw]))
                    p.wait("tensor", "vector", ev)
                for si in range(sw // 128):
                    for dc in range(8):
                        pe = dc % 2
                        p.wait("tensor", "vector", p.cnt["vector"])
                        mmv = p.mm(lambda h, dc=dc, si=si: h.transpose(
                            ps_tf[:], sb_eo[:, dc * 256 + si * 128: dc * 256 + (si + 1) * 128],
                            sb_if[:]))
                        p.wait("vector", "tensor", mmv)
                        p.wait("vector", "gpsimd", p.cnt["gpsimd"])
                        ev = p.op("vector", lambda h, pe=pe: h.tensor_copy(
                            sb_eot[:, pe * 128:(pe + 1) * 128], ps_tf[:]))
                        p.wait("gpsimd", "vector", ev)
                        p.dma("gpsimd", lambda h, slot0=slot0, si=si, dc=dc, pe=pe: h.dma_start(
                            out=o_eo[slot0 + si * 128: slot0 + (si + 1) * 128,
                                     dc * 128:(dc + 1) * 128],
                            in_=sb_eot[:, pe * 128:(pe + 1) * 128]))
        p.barrier_out("gpsimd")
        p.build(nc)
    return nc


def host_L4(inputs, cache, run):
    xn_full = np.ascontiguousarray(cache["xn_all"].astype(np.float32))
    key_pt = cache["key_pt"]
    NT = S // 128
    iotaN = np.tile(np.arange(2 * CAP, dtype=np.float32), (128, 1))
    lhs3 = np.zeros((128, 3 * NT), np.float32)
    for t in range(NT):
        lhs3[:, 3 * t] = np.arange(128)
        lhs3[:, 3 * t + 1] = t % 16
        lhs3[:, 3 * t + 2] = t // 16
    lhs3 = lhs3.astype(_bf16())
    w1 = np.asarray(inputs["L0_ffn_w1"], np.float32)
    w2 = np.asarray(inputs["L0_ffn_w2"], np.float32)
    idenb = np.eye(128, dtype=np.float32).astype(_bf16())
    idenf = np.eye(128, dtype=np.float32)
    nc = build_L4()
    in_maps = []
    for c in range(NC):
        in_maps.append(dict(
            xn_full=xn_full, key_pt=key_pt, iotaN=iotaN, lhs3=lhs3,
            coef3=np.array([[1.0], [128.0], [2048.0]], np.float32).astype(_bf16()),
            base_col=np.full((128, 1), c * 2 * CAP, np.float32),
            w1=np.ascontiguousarray(w1[2 * c:2 * c + 2]),
            w2=np.ascontiguousarray(w2[2 * c:2 * c + 2]),
            idenb=idenb, idenf=idenf))
    res = run(nc, in_maps)
    cache["eo_all"] = np.concatenate([np.asarray(res[c]["eo"]) for c in range(NC)], 0)


def build_L5():
    nc = bass.Bass("TRN2", target_bir_lowering=False, debug=False)
    eo_all = nc.dram_tensor("eo_all", [NEXP * CAP, E], F32, kind="ExternalInput").ap()
    key_pt = nc.dram_tensor("key_pt", [128, TPC // 128], F32, kind="ExternalInput").ap()
    prob_pt = nc.dram_tensor("prob_pt", [128, TPC // 128], F32, kind="ExternalInput").ap()
    ys_fm = nc.dram_tensor("ys_fm", [E, TPC], F32, kind="ExternalInput").ap()
    z_fm = nc.dram_tensor("z_fm", [E, TPC], F32, kind="ExternalInput").ap()
    x_fm = nc.dram_tensor("x_fm", [E, TPC], F32, kind="ExternalInput").ap()
    woT = nc.dram_tensor("woT", [E, E], F32, kind="ExternalInput").ap()
    idenf = nc.dram_tensor("idenf", [128, 128], F32, kind="ExternalInput").ap()

    o_x1 = nc.dram_tensor("x1", [E, TPC], F32, kind="ExternalOutput").ap()

    p = Prog()
    from contextlib import ExitStack
    with ExitStack() as st:
        En = st.enter_context
        NT = TPC // 128
        sb_key = En(nc.sbuf_tensor("sb_key", [128, NT], F32))
        sb_ki = En(nc.sbuf_tensor("sb_ki", [128, NT], I32))
        sb_prob = En(nc.sbuf_tensor("sb_prob", [128, NT], F32))
        sb_rows = En(nc.sbuf_tensor("sb_rows", [128, E], F32))
        sb_grow = En(nc.sbuf_tensor("sb_grow", [128, E], F32))
        sb_if = En(nc.sbuf_tensor("sb_if", [128, 128], F32))
        sb_g = En(nc.sbuf_tensor("sb_g", [128, 8 * TPC], F32))
        sb_zc = En(nc.sbuf_tensor("sb_zc", [128, TPC], F32))
        sb_yc = En(nc.sbuf_tensor("sb_yc", [128, TPC], F32))
        sb_ygb = En(nc.sbuf_tensor("sb_ygb", [128, 8 * TPC], BF16))
        sb_w = En(nc.sbuf_tensor("sb_w", [128, E], F32))
        sb_wb = En(nc.sbuf_tensor("sb_wb", [128, E], BF16))
        sb_xc = En(nc.sbuf_tensor("sb_xc", [128, TPC], F32))
        sb_o = En(nc.sbuf_tensor("sb_o", [128, 512], F32))
        ps_tf = En(nc.psum_tensor("ps_tf", [128, 128], F32))
        ps_o = En(nc.psum_tensor("ps_o", [128, 512], F32))

        for f in [lambda h: h.dma_start(out=sb_key[:], in_=key_pt),
                  lambda h: h.dma_start(out=sb_prob[:], in_=prob_pt),
                  lambda h: h.dma_start(out=sb_if[:], in_=idenf)]:
            p.dma("sync", f)
        prepld = p.cnt["sync"]
        for e in ("scalar", "vector", "tensor"):
            p.wait(e, "sync", prepld)
        p.op("vector", lambda h: h.tensor_copy(sb_ki[:], sb_key[:]))
        p.op("vector", lambda h: h.memset(sb_rows[:], 0.0))

        for t in range(NT):
            p.wait("gpsimd", "vector", p.cnt["vector"])
            gld = p.dma("gpsimd", lambda h, t=t: h.indirect_dma_start(
                out=sb_rows[:], out_offset=None, in_=eo_all,
                in_offset=bass.IndirectOffsetOnAxis(ap=sb_ki[:, t:t + 1], axis=0),
                oob_is_err=False, bounds_check=NEXP * CAP))
            p.wait("scalar", "gpsimd", gld)
            sv = p.op("scalar", lambda h, t=t: h.activation(
                out=sb_grow[:], in_=sb_rows[:], func=AF.Silu, scale=sb_prob[:, t:t + 1]))
            p.wait("tensor", "scalar", sv)
            for d in range(8):
                p.wait("tensor", "vector", p.cnt["vector"])
                mmv = p.mm(lambda h, d=d: h.transpose(
                    ps_tf[:], sb_grow[:, d * 128:(d + 1) * 128], sb_if[:]))
                p.wait("vector", "tensor", mmv)
                p.op("vector", lambda h, d=d, t=t: h.tensor_copy(
                    sb_g[:, d * TPC + t * 128: d * TPC + (t + 1) * 128], ps_tf[:]))
            p.wait("gpsimd", "vector", p.cnt["vector"])
            p.wait("scalar", "vector", p.cnt["vector"])

        for d in range(8):
            zld = p.dma("sync", lambda h, d=d: h.dma_start(
                out=sb_zc[:], in_=z_fm[d * 128:(d + 1) * 128, :]))
            p.wait("scalar", "sync", zld)
            sv = p.op("scalar", lambda h: h.activation(out=sb_zc[:], in_=sb_zc[:], func=AF.Silu))
            yld = p.dma("sync", lambda h, d=d: h.dma_start(
                out=sb_yc[:], in_=ys_fm[d * 128:(d + 1) * 128, :]))
            p.wait("vector", "sync", yld)
            p.wait("vector", "scalar", sv)
            yv = p.op("vector", lambda h, d=d: h.tensor_tensor(
                out=sb_ygb[:, d * TPC:(d + 1) * TPC], in0=sb_yc[:], in1=sb_zc[:], op=ALU.mult))
            p.wait("sync", "vector", yv)

        yg_ready = p.cnt["vector"]

        last_ev = 0
        for oc in range(8):
            p.wait("sync", "vector", last_ev)
            wld = p.dma("sync", lambda h, oc=oc: h.dma_start(
                out=sb_w[:, 0:E].rearrange("p (c f) -> p c f", c=8),
                in_=woT[:, oc * 128:(oc + 1) * 128].rearrange("(c p) f -> p c f", p=128)))
            p.wait("vector", "sync", wld)
            cv = p.op("vector", lambda h: h.tensor_copy(sb_wb[:], sb_w[:]))
            xld = p.dma("sync", lambda h, oc=oc: h.dma_start(
                out=sb_xc[:], in_=x_fm[oc * 128:(oc + 1) * 128, :]))
            p.wait("tensor", "sync", xld)
            for ntile in range(TPC // 512):
                p.wait("tensor", "vector", max(cv, yg_ready, last_ev))
                fs = []
                for d in range(8):
                    def mmf(h, d=d, oc=oc, ntile=ntile):
                        return h.matmul(
                            ps_o[:],
                            sb_wb[:, d * 128:(d + 1) * 128],
                            sb_ygb[:, d * TPC + ntile * 512: d * TPC + (ntile + 1) * 512],
                            start=(d == 0), stop=(d == 7))
                    fs.append(mmf)
                mmv = p.mm_group(fs)
                p.wait("vector", "tensor", mmv)
                p.wait("vector", "gpsimd", p.cnt["gpsimd"])
                p.op("vector", lambda h, oc=oc, ntile=ntile: h.tensor_tensor(
                    out=sb_o[:], in0=ps_o[:],
                    in1=sb_g[:, oc * TPC + ntile * 512: oc * TPC + (ntile + 1) * 512],
                    op=ALU.mult))
                last_ev = p.op("vector", lambda h, ntile=ntile: h.tensor_tensor(
                    out=sb_o[:], in0=sb_o[:, 0:512],
                    in1=sb_xc[:, ntile * 512:(ntile + 1) * 512], op=ALU.add))
                p.wait("gpsimd", "vector", last_ev)
                p.dma("gpsimd", lambda h, oc=oc, ntile=ntile: h.dma_start(
                    out=o_x1[oc * 128:(oc + 1) * 128, ntile * 512:(ntile + 1) * 512],
                    in_=sb_o[:]))
        p.barrier_out("gpsimd")
        p.build(nc)
    return nc


def host_L5(inputs, cache, run):
    x = np.asarray(inputs["L0_attn_x"], np.float32).reshape(S, E)
    xT = np.ascontiguousarray(x.T)
    woT = np.ascontiguousarray(np.asarray(inputs["L0_attn_W_out"], np.float32).T)
    idenf = np.eye(128, dtype=np.float32)
    key_f = cache["key_flat"]
    prob_f = cache["prob_flat"]
    nc = build_L5()
    in_maps = []
    for c in range(NC):
        tok = slice(c * TPC, (c + 1) * TPC)
        in_maps.append(dict(
            eo_all=cache["eo_all"],
            key_pt=np.ascontiguousarray(key_f[tok].reshape(TPC // 128, 128).T),
            prob_pt=np.ascontiguousarray(prob_f[tok].reshape(TPC // 128, 128).T),
            ys_fm=np.ascontiguousarray(cache["yssm_all"][:, tok]),
            z_fm=np.ascontiguousarray(cache["z_all"][:, tok]),
            x_fm=np.ascontiguousarray(xT[:, tok]),
            woT=woT, idenf=idenf))
    res = run(nc, in_maps)
    cache["x1_all"] = np.concatenate([np.asarray(res[c]["x1"]) for c in range(NC)], 1)


def build_L6():
    nc = bass.Bass("TRN2", target_bir_lowering=False, debug=False)
    x_tok = nc.dram_tensor("x_tok", [TPC, E], F32, kind="ExternalInput").ap()
    wkvd = nc.dram_tensor("wkvd", [E, KVR], F32, kind="ExternalInput").ap()
    wq = nc.dram_tensor("wq", [E, HD], F32, kind="ExternalInput").ap()
    wkvu = nc.dram_tensor("wkvu", [KVR, 2 * HD], F32, kind="ExternalInput").ap()
    iden = nc.dram_tensor("iden", [128, 128], BF16, kind="ExternalInput").ap()

    NT = TPC // 128
    o_q = nc.dram_tensor("q", [HD, TPC], F32, kind="ExternalOutput").ap()
    o_k = nc.dram_tensor("k", [HD, TPC], F32, kind="ExternalOutput").ap()
    o_v = nc.dram_tensor("v", [TPC, HD], F32, kind="ExternalOutput").ap()
    o_m = nc.dram_tensor("m", [128, NT], F32, kind="ExternalOutput").ap()
    o_r = nc.dram_tensor("r", [128, NT], F32, kind="ExternalOutput").ap()
    o_vr = nc.dram_tensor("vr", [128, NT], F32, kind="ExternalOutput").ap()

    p = Prog()
    from contextlib import ExitStack
    with ExitStack() as st:
        En = st.enter_context
        sb_x = En(nc.sbuf_tensor("sb_x", [128, E], F32))
        sb_x2 = En(nc.sbuf_tensor("sb_x2", [128, E], F32))
        sb_xn = En(nc.sbuf_tensor("sb_xn", [128, E], BF16))
        sb_m = En(nc.sbuf_tensor("sb_m", [128, NT], F32))
        sb_r = En(nc.sbuf_tensor("sb_r", [128, NT], F32))
        sb_vr = En(nc.sbuf_tensor("sb_vr", [128, NT], F32))
        sb_t1 = En(nc.sbuf_tensor("sb_t1", [128, 1], F32))
        sb_t2 = En(nc.sbuf_tensor("sb_t2", [128, 1], F32))
        sb_iden = En(nc.sbuf_tensor("sb_iden", [128, 128], BF16))
        sb_xnfm = En(nc.sbuf_tensor("sb_xnfm", [128, 8 * TPC], BF16))
        sb_w = En(nc.sbuf_tensor("sb_w", [128, 1024], F32))
        sb_wb = En(nc.sbuf_tensor("sb_wb", [128, 1024], BF16))
        sb_lat = En(nc.sbuf_tensor("sb_lat", [128, TPC], F32))
        sb_latb = En(nc.sbuf_tensor("sb_latb", [128, TPC], BF16))
        sb_wub = En(nc.sbuf_tensor("sb_wub", [128, 2 * HD], BF16))
        sb_out = En(nc.sbuf_tensor("sb_out", [128, 512], F32))
        ps_tp = En(nc.psum_tensor("ps_tp", [128, 128], BF16))
        ps_o = En(nc.psum_tensor("ps_o", [128, 512], F32))

        ild = p.dma("sync", lambda h: h.dma_start(out=sb_iden[:], in_=iden))
        p.wait("tensor", "sync", ild)

        last_xn = 0
        last_sq = 0
        for t in range(NT):
            p.wait("sync", "vector", last_xn)
            p.wait("sync", "scalar", last_sq)
            ld = p.dma("sync", lambda h, t=t: h.dma_start(
                out=sb_x[:], in_=x_tok[t * 128:(t + 1) * 128, :]))
            p.wait("vector", "sync", ld)
            p.wait("scalar", "sync", ld)
            p.op("vector", lambda h, t=t: h.reduce_sum(sb_m[:, t:t + 1], sb_x[:], axis=AX.X))
            p.op("vector", lambda h, t=t: h.tensor_scalar_mul(sb_m[:, t:t + 1], sb_m[:, t:t + 1], 1.0 / E))
            last_sq = p.op("scalar", lambda h: h.activation(out=sb_x2[:], in_=sb_x[:], func=AF.Square))
            p.wait("vector", "scalar", last_sq)
            p.op("vector", lambda h: h.reduce_sum(sb_t1[:], sb_x2[:], axis=AX.X))
            p.op("vector", lambda h, t=t: h.tensor_single_scalar(
                out=sb_t2[:], in_=sb_m[:, t:t + 1], scalar=sb_m[:, t:t + 1], op=ALU.mult))
            p.op("vector", lambda h, t=t: h.tensor_scalar(
                out=sb_vr[:, t:t + 1], in0=sb_t1[:], scalar1=1.0 / E, scalar2=sb_t2[:],
                op0=ALU.mult, op1=ALU.subtract))
            vs = p.op("vector", lambda h, t=t: h.tensor_scalar_add(sb_t1[:], sb_vr[:, t:t + 1], 1e-5))
            p.wait("scalar", "vector", vs)
            p.op("scalar", lambda h: h.activation(out=sb_t2[:], in_=sb_t1[:], func=AF.Ln))
            sc = p.op("scalar", lambda h, t=t: h.activation(
                out=sb_r[:, t:t + 1], in_=sb_t2[:], func=AF.Exp, scale=-0.5))
            p.wait("vector", "scalar", sc)
            last_xn = p.op("vector", lambda h, t=t: h.tensor_scalar(
                out=sb_xn[:], in0=sb_x[:], scalar1=sb_m[:, t:t + 1],
                scalar2=sb_r[:, t:t + 1], op0=ALU.subtract, op1=ALU.mult))
            p.wait("tensor", "vector", last_xn)
            for d in range(8):
                p.wait("tensor", "vector", last_xn if d == 0 else p.cnt["vector"])
                p.mm(lambda h, d=d: h.transpose(
                    ps_tp[:], sb_xn[:, d * 128:(d + 1) * 128], sb_iden[:]))
                p.wait("vector", "tensor", p.cnt["tensor"])
                last_xn = p.op("vector", lambda h, d=d, t=t: h.tensor_copy(
                    sb_xnfm[:, d * TPC + t * 128: d * TPC + (t + 1) * 128], ps_tp[:]))

        vs = p.cnt["vector"]
        p.wait("gpsimd", "vector", vs)
        p.dma("gpsimd", lambda h: h.dma_start(out=o_m, in_=sb_m[:]))
        p.dma("gpsimd", lambda h: h.dma_start(out=o_r, in_=sb_r[:]))
        p.dma("gpsimd", lambda h: h.dma_start(out=o_vr, in_=sb_vr[:]))
        xn_ready = p.cnt["vector"]

        emst = {"cv": 0, "ev": 0}

        def emit_mm(wT_ap, FDIM, out_ap, to_lat=False):
            for fc in range(ceil_div(FDIM, 128)):
                fw = min(128, FDIM - fc * 128)
                p.wait("sync", "vector", emst["cv"])
                wld = p.dma("sync", lambda h, fc=fc, fw=fw: h.dma_start(
                    out=sb_w[:, 0:8 * fw].rearrange("p (c f) -> p c f", c=8),
                    in_=wT_ap[:, fc * 128:fc * 128 + fw].rearrange("(c p) f -> p c f", p=128)))
                p.wait("vector", "sync", wld)
                emst["cv"] = p.op("vector", lambda h, fw=fw: h.tensor_copy(
                    sb_wb[:, 0:8 * fw], sb_w[:, 0:8 * fw]))
                for ntile in range(TPC // 512):
                    p.wait("tensor", "vector", max(emst["cv"], xn_ready, emst["ev"]))
                    fs = []
                    for d in range(8):
                        def mmf(h, d=d, fw=fw, ntile=ntile):
                            return h.matmul(
                                ps_o[0:fw, :],
                                sb_wb[:, d * fw:(d + 1) * fw],
                                sb_xnfm[:, d * TPC + ntile * 512: d * TPC + (ntile + 1) * 512],
                                start=(d == 0), stop=(d == 7))
                        fs.append(mmf)
                    mmdone = p.mm_group(fs)
                    p.wait("vector", "tensor", mmdone)
                    p.wait("vector", "gpsimd", p.cnt["gpsimd"])
                    if to_lat:
                        emst["ev"] = p.op("vector", lambda h, ntile=ntile: h.tensor_copy(
                            sb_lat[:, ntile * 512:(ntile + 1) * 512], ps_o[0:128, :]))
                    else:
                        emst["ev"] = p.op("vector", lambda h, fw=fw: h.tensor_copy(
                            sb_out[0:fw, 0:512], ps_o[0:fw, :]))
                        p.wait("gpsimd", "vector", emst["ev"])
                        p.dma("gpsimd", lambda h, fc=fc, fw=fw, ntile=ntile, out_ap=out_ap: h.dma_start(
                            out=out_ap[fc * 128:fc * 128 + fw, ntile * 512:(ntile + 1) * 512],
                            in_=sb_out[0:fw, 0:512]))

        emit_mm(wkvd, KVR, None, to_lat=True)
        lv = p.op("vector", lambda h: h.tensor_copy(sb_latb[:], sb_lat[:]))
        emit_mm(wq, HD, o_q)

        for half in range(2):
            p.wait("sync", "vector", emst["cv"])
            wul = p.dma("sync", lambda h, half=half: h.dma_start(
                out=sb_w[:, 0:HD], in_=wkvu[:, half * HD:(half + 1) * HD]))
            p.wait("vector", "sync", wul)
            emst["cv"] = p.op("vector", lambda h, half=half: h.tensor_copy(
                sb_wub[:, half * HD:(half + 1) * HD], sb_w[:, 0:HD]))
        wuc = emst["cv"]
        p.wait("tensor", "vector", max(wuc, lv))
        for fc in range(8):
            for ntile in range(TPC // 512):
                p.wait("tensor", "vector", max(emst["ev"], wuc, lv))
                mmdone = p.mm(lambda h, fc=fc, ntile=ntile: h.matmul(
                    ps_o[:], sb_wub[:, fc * 128:(fc + 1) * 128],
                    sb_latb[:, ntile * 512:(ntile + 1) * 512], start=True, stop=True))
                p.wait("vector", "tensor", mmdone)
                p.wait("vector", "gpsimd", p.cnt["gpsimd"])
                emst["ev"] = p.op("vector", lambda h: h.tensor_copy(sb_out[:, 0:512], ps_o[:]))
                p.wait("gpsimd", "vector", emst["ev"])
                p.dma("gpsimd", lambda h, fc=fc, ntile=ntile: h.dma_start(
                    out=o_k[fc * 128:(fc + 1) * 128, ntile * 512:(ntile + 1) * 512],
                    in_=sb_out[:, 0:512]))
        for tb in range(NT):
            for nn in range(2):
                p.wait("tensor", "vector", emst["ev"])
                mmdone = p.mm(lambda h, tb=tb, nn=nn: h.matmul(
                    ps_o[:], sb_latb[:, tb * 128:(tb + 1) * 128],
                    sb_wub[:, HD + nn * 512: HD + (nn + 1) * 512], start=True, stop=True))
                p.wait("vector", "tensor", mmdone)
                p.wait("vector", "gpsimd", p.cnt["gpsimd"])
                emst["ev"] = p.op("vector", lambda h: h.tensor_copy(sb_out[:, 0:512], ps_o[:]))
                p.wait("gpsimd", "vector", emst["ev"])
                p.dma("gpsimd", lambda h, tb=tb, nn=nn: h.dma_start(
                    out=o_v[tb * 128:(tb + 1) * 128, nn * 512:(nn + 1) * 512],
                    in_=sb_out[:, 0:512]))
        p.barrier_out("gpsimd")
        p.build(nc)
    return nc


def host_L6(inputs, cache, run):
    x1T = cache["x1_all"]
    wkvd = np.ascontiguousarray(np.asarray(inputs["L1_attn_w_kv_down"], np.float32))
    wq = np.ascontiguousarray(np.asarray(inputs["L1_attn_w_q"], np.float32))
    wkvu = np.ascontiguousarray(np.asarray(inputs["L1_attn_w_kv_up"], np.float32))
    iden = np.eye(128, dtype=np.float32).astype(_bf16())
    nc = build_L6()
    in_maps = []
    for c in range(NC):
        tok = slice(c * TPC, (c + 1) * TPC)
        in_maps.append(dict(
            x_tok=np.ascontiguousarray(x1T[:, tok].T),
            wkvd=wkvd, wq=wq, wkvu=wkvu, iden=iden))
    res = run(nc, in_maps)
    cache["q_all"] = np.concatenate([np.asarray(res[c]["q"]) for c in range(NC)], 1)
    cache["k_all"] = np.concatenate([np.asarray(res[c]["k"]) for c in range(NC)], 1)
    cache["v_all"] = np.concatenate([np.asarray(res[c]["v"]) for c in range(NC)], 0)
    cache["m1_all"] = np.concatenate([np.asarray(res[c]["m"]).T.reshape(-1) for c in range(NC)]).astype(np.float32)
    cache["r1_all"] = np.concatenate([np.asarray(res[c]["r"]).T.reshape(-1) for c in range(NC)]).astype(np.float32)
    cache["v1_all"] = np.concatenate([np.asarray(res[c]["vr"]).T.reshape(-1) for c in range(NC)]).astype(np.float32)


def build_L7():
    nc = bass.Bass("TRN2", target_bir_lowering=False, debug=False)
    q_in = nc.dram_tensor("q_in", [512, L], F32, kind="ExternalInput").ap()
    k_in = nc.dram_tensor("k_in", [512, L], F32, kind="ExternalInput").ap()
    v_in = nc.dram_tensor("v_in", [L, 512], F32, kind="ExternalInput").ap()
    masks = nc.dram_tensor("masks", [128, 8 * 1024], BF16, kind="ExternalInput").ap()
    onesc = nc.dram_tensor("onesc", [128, 1], BF16, kind="ExternalInput").ap()
    onesf = nc.dram_tensor("onesf", [64, 1], F32, kind="ExternalInput").ap()
    ones1f = nc.dram_tensor("ones1f", [1, 64], F32, kind="ExternalInput").ap()

    o_y = nc.dram_tensor("y", [512, L], F32, kind="ExternalOutput").ap()

    p = Prog()
    from contextlib import ExitStack
    with ExitStack() as st:
        En = st.enter_context
        sb_st = En(nc.sbuf_tensor("sb_st", [128, L // 2], F32))
        sb_q = En(nc.sbuf_tensor("sb_q", [128, 4 * L], BF16))
        sb_k = En(nc.sbuf_tensor("sb_k", [128, 4 * L], BF16))
        sb_v = En(nc.sbuf_tensor("sb_v", [128, 32 * 512], BF16))
        sb_sq = En(nc.sbuf_tensor("sb_sq", [128, L // 2], F32))
        sb_qn = En(nc.sbuf_tensor("sb_qn", [1, L], F32))
        sb_kn = En(nc.sbuf_tensor("sb_kn", [1, L], F32))
        sb_nb = En(nc.sbuf_tensor("sb_nb", [1, L], BF16))
        sb_km = En(nc.sbuf_tensor("sb_km", [1, 1], F32))
        sb_kext = En(nc.sbuf_tensor("sb_kext", [65, L], BF16))
        sb_qext = En(nc.sbuf_tensor("sb_qext", [65, L], BF16))
        sb_p = En(nc.sbuf_tensor("sb_p", [128, 2 * 1024], BF16))
        sb_msk = En(nc.sbuf_tensor("sb_msk", [128, 8 * 1024], BF16))
        sb_oc = En(nc.sbuf_tensor("sb_oc", [128, 1], BF16))
        sb_of = En(nc.sbuf_tensor("sb_of", [64, 1], F32))
        sb_o1f = En(nc.sbuf_tensor("sb_o1f", [1, 64], F32))
        sb_rr = En(nc.sbuf_tensor("sb_rr", [1, 1024], F32))
        sb_yo = En(nc.sbuf_tensor("sb_yo", [64, 1024], F32))
        sb_vxb = En(nc.sbuf_tensor("sb_vxb", [128, 32 * 65], BF16))
        ps_s = En(nc.psum_tensor("ps_s", [128, 1024], F32))
        ps_s2 = En(nc.psum_tensor("ps_s2", [128, 1024], F32))
        ps_ya = En(nc.psum_tensor("ps_ya", [65, 512], F32))
        ps_yb = En(nc.psum_tensor("ps_yb", [65, 512], F32))
        ps_n = En(nc.psum_tensor("ps_n", [1, 512], F32))
        ps_rb = En(nc.psum_tensor("ps_rb", [64, 512], F32))

        p.dma("sync", lambda h: h.dma_start(out=sb_msk[:], in_=masks))
        p.dma("sync", lambda h: h.dma_start(out=sb_oc[:], in_=onesc))
        p.dma("sync", lambda h: h.dma_start(out=sb_of[:], in_=onesf))
        prepld = p.dma("sync", lambda h: h.dma_start(out=sb_o1f[:], in_=ones1f))
        for e in ("scalar", "vector", "tensor"):
            p.wait(e, "sync", prepld)

        lastv = 0
        HL = L // 2
        for i in range(4):
            for hf in range(2):
                ld = p.dma("sync", lambda h, i=i, hf=hf: h.dma_start(
                    out=sb_st[:], in_=q_in[i * 128:(i + 1) * 128, hf * HL:(hf + 1) * HL]))
                p.wait("vector", "sync", ld)
                lastv = p.op("vector", lambda h, i=i, hf=hf: h.tensor_single_scalar(
                    out=sb_q[:, i * L + hf * HL: i * L + (hf + 1) * HL], in_=sb_st[:],
                    scalar=0.125, op=ALU.mult))
                p.wait("sync", "vector", lastv)
        for i in range(4):
            for hf in range(2):
                ld = p.dma("sync", lambda h, i=i, hf=hf: h.dma_start(
                    out=sb_st[:], in_=k_in[i * 128:(i + 1) * 128, hf * HL:(hf + 1) * HL]))
                p.wait("vector", "sync", ld)
                lastv = p.op("vector", lambda h, i=i, hf=hf: h.tensor_copy(
                    sb_k[:, i * L + hf * HL: i * L + (hf + 1) * HL], sb_st[:]))
                p.wait("sync", "vector", lastv)
        for i in range(32):
            ld = p.dma("sync", lambda h, i=i: h.dma_start(
                out=sb_st[:, 0:512], in_=v_in[i * 128:(i + 1) * 128, :]))
            p.wait("vector", "sync", ld)
            lastv = p.op("vector", lambda h, i=i: h.tensor_copy(
                sb_v[:, i * 512:(i + 1) * 512], sb_st[:, 0:512]))
            p.wait("sync", "vector", lastv)

        for hh in range(8):
            ti = hh // 2
            rb = (hh % 2) * 64
            qh = lambda sl: sb_q[rb:rb + 64, ti * L + sl.start: ti * L + sl.stop] if False else None
            for src_t, dst in ((sb_q, sb_qn), (sb_k, sb_kn)):
                for half in range(2):
                    p.fence(("scalar", "vector", "tensor"))
                    p.op("scalar", lambda h, ti=ti, rb=rb, src_t=src_t, half=half: h.activation(
                        out=sb_sq[0:64, :],
                        in_=src_t[rb:rb + 64, ti * L + half * (L // 2): ti * L + (half + 1) * (L // 2)],
                        func=AF.Square))
                    p.fence(("scalar", "tensor"))
                    for ch in range(4):
                        mmv = p.mm(lambda h, ch=ch: h.matmul(
                            ps_n[:], sb_of[:], sb_sq[0:64, ch * 512:(ch + 1) * 512],
                            start=True, stop=True))
                        p.wait("vector", "tensor", mmv)
                        ev = p.op("vector", lambda h, ch=ch, dst=dst, half=half: h.tensor_copy(
                            dst[:, half * (L // 2) + ch * 512: half * (L // 2) + (ch + 1) * 512], ps_n[:]))
                        p.wait("tensor", "vector", ev)
            kv = p.op("vector", lambda h: h.reduce_max(sb_km[:], sb_kn[:], axis=AX.X))
            p.wait("scalar", "vector", kv)
            sv = p.op("scalar", lambda h: h.activation(out=sb_km[:], in_=sb_km[:], func=AF.Sqrt))
            sv = p.op("scalar", lambda h: h.activation(out=sb_qn[:], in_=sb_qn[:], func=AF.Sqrt))
            p.wait("vector", "scalar", sv)
            p.op("vector", lambda h: h.tensor_scalar(
                out=sb_nb[:], in0=sb_qn[:], scalar1=sb_km[:], scalar2=-0.125,
                op0=ALU.mult, op1=ALU.mult))
            p.op("vector", lambda h: h.memset(sb_kext[64:65, :], 1.0))
            p.op("vector", lambda h, ti=ti, rb=rb: h.tensor_copy(
                sb_kext[0:64, :], sb_k[rb:rb + 64, ti * L:(ti + 1) * L]))
            p.op("vector", lambda h, ti=ti, rb=rb: h.tensor_copy(
                sb_qext[0:64, :], sb_q[rb:rb + 64, ti * L:(ti + 1) * L]))
            bv = p.op("vector", lambda h: h.tensor_copy(sb_qext[64:65, :], sb_nb[:]))
            p.wait("tensor", "vector", bv)
            p.wait("scalar", "vector", bv)

            p.fence(("vector", "tensor"))
            for kb in range(32):
                p.op("vector", lambda h, kb=kb, hh=hh: h.tensor_copy(
                    sb_vxb[:, kb * 65: kb * 65 + 64],
                    sb_v.rearrange("p (kb x) -> p kb x", x=512)[:, kb, hh * 64:(hh + 1) * 64]))
            vxv = p.op("vector", lambda h: h.memset(
                sb_vxb.rearrange("p (kb x) -> p kb x", x=65)[:, :, 64:65], 1.0))
            p.wait("tensor", "vector", vxv)

            for qc in range(4):
                nkb = 8 * (qc + 1)
                sc_done = {}
                mm_of = {}

                def emit_mm(kb, qc=qc):
                    par = kb % 2
                    pss = ps_s if par == 0 else ps_s2
                    if kb >= 2:
                        p.wait("tensor", "scalar", sc_done[kb - 2])
                    p.mm(lambda h, kb=kb, qc=qc, pss=pss: h.matmul(
                        pss[:, 0:512], sb_kext[:, kb * 128:(kb + 1) * 128],
                        sb_qext[:, qc * 1024:qc * 1024 + 512], start=True, stop=True))
                    mm_of[kb] = p.mm(lambda h, kb=kb, qc=qc, pss=pss: h.matmul(
                        pss[:, 512:1024], sb_kext[:, kb * 128:(kb + 1) * 128],
                        sb_qext[:, qc * 1024 + 512:(qc + 1) * 1024], start=True, stop=True))

                emit_mm(0)
                for kb in range(nkb):
                    par = kb % 2
                    pss = ps_s if par == 0 else ps_s2
                    pofs = par * 1024
                    if kb + 1 < nkb:
                        emit_mm(kb + 1)
                    p.wait("scalar", "tensor", mm_of[kb])
                    sv = p.op("scalar", lambda h, pss=pss, pofs=pofs: h.activation(
                        out=sb_p[:, pofs:pofs + 1024], in_=pss[:], func=AF.Exp))
                    sc_done[kb] = sv
                    if kb >= 8 * qc:
                        vi = kb - 8 * qc
                        p.wait("vector", "scalar", sv)
                        dv = p.op("vector", lambda h, vi=vi, pofs=pofs: h.tensor_tensor(
                            out=sb_p[:, pofs:pofs + 1024], in0=sb_p[:, pofs:pofs + 1024],
                            in1=sb_msk[:, vi * 1024:(vi + 1) * 1024], op=ALU.mult))
                        p.wait("tensor", "vector", dv)
                    else:
                        p.wait("tensor", "scalar", sv)
                    p.mm(lambda h, kb=kb, pofs=pofs: h.matmul(
                        ps_ya[:], sb_vxb[:, kb * 65:(kb + 1) * 65],
                        sb_p[:, pofs:pofs + 512], start=(kb == 0), stop=(kb == nkb - 1),
                        skip_group_check=True))
                    p.mm(lambda h, kb=kb, pofs=pofs: h.matmul(
                        ps_yb[:], sb_vxb[:, kb * 65:(kb + 1) * 65],
                        sb_p[:, pofs + 512:pofs + 1024], start=(kb == 0), stop=(kb == nkb - 1),
                        skip_group_check=True))
                mmv = p.cnt["tensor"]
                p.wait("vector", "tensor", mmv)
                p.op("vector", lambda h: h.reciprocal(sb_rr[0:1, 0:512], ps_ya[64:65, :]))
                p.op("vector", lambda h: h.reciprocal(sb_rr[0:1, 512:1024], ps_yb[64:65, :]))
                p.op("vector", lambda h: h.tensor_copy(sb_yo[:, 0:512], ps_ya[0:64, :]))
                ev = p.op("vector", lambda h: h.tensor_copy(sb_yo[:, 512:1024], ps_yb[0:64, :]))
                p.wait("tensor", "vector", ev)
                for half in range(2):
                    mmv = p.mm(lambda h, half=half: h.matmul(
                        ps_rb[:], sb_o1f[:], sb_rr[0:1, half * 512:(half + 1) * 512],
                        start=True, stop=True))
                    p.wait("vector", "tensor", mmv)
                    yv = p.op("vector", lambda h, half=half: h.tensor_tensor(
                        out=sb_yo[:, half * 512:(half + 1) * 512],
                        in0=sb_yo[:, half * 512:(half + 1) * 512], in1=ps_rb[:], op=ALU.mult))
                    p.wait("tensor", "vector", yv)
                yv = p.cnt["vector"]
                p.wait("gpsimd", "vector", yv)
                p.wait("gpsimd", "vector", yv)
                p.dma("gpsimd", lambda h, hh=hh, qc=qc: h.dma_start(
                    out=o_y[hh * 64:(hh + 1) * 64, qc * 1024:(qc + 1) * 1024],
                    in_=sb_yo[:]))
                p.wait("vector", "gpsimd", p.cnt["gpsimd"])
                p.wait("tensor", "gpsimd", p.cnt["gpsimd"])
        p.barrier_out("gpsimd")
        p.build(nc)
    return nc


def host_L7(inputs, cache, run):
    q = cache["q_all"]
    k = cache["k_all"]
    v = cache["v_all"]
    masks = np.zeros((128, 8 * 1024), np.float32)
    for vi in range(8):
        kk = np.arange(128)[:, None]
        qq = np.arange(1024)[None, :]
        masks[:, vi * 1024:(vi + 1) * 1024] = (kk <= qq - vi * 128)
    masks = masks.astype(_bf16())
    onesc = np.ones((128, 1), np.float32).astype(_bf16())
    onesf = np.ones((64, 1), np.float32)
    ones1f = np.ones((1, 64), np.float32)
    nc = build_L7()
    in_maps = []
    for c in range(NC):
        b = c // 2
        hg = c % 2
        rows = slice(hg * 512, (hg + 1) * 512)
        cols = slice(b * L, (b + 1) * L)
        in_maps.append(dict(
            q_in=np.ascontiguousarray(q[rows, cols]),
            k_in=np.ascontiguousarray(k[rows, cols]),
            v_in=np.ascontiguousarray(v[cols, rows]),
            masks=masks, onesc=onesc, onesf=onesf, ones1f=ones1f))
    res = run(nc, in_maps)
    y = np.zeros((HD, S), np.float32)
    for c in range(NC):
        b = c // 2
        hg = c % 2
        y[hg * 512:(hg + 1) * 512, b * L:(b + 1) * L] = np.asarray(res[c]["y"])
    cache["yatt_all"] = y


def build_L8():
    nc = bass.Bass("TRN2", target_bir_lowering=False, debug=False)
    x1_fm = nc.dram_tensor("x1_fm", [E, TPC], F32, kind="ExternalInput").ap()
    yat_fm = nc.dram_tensor("yat_fm", [HD, TPC], F32, kind="ExternalInput").ap()
    wupT = nc.dram_tensor("wupT", [E, DFF], F32, kind="ExternalInput").ap()
    wdnT = nc.dram_tensor("wdnT", [DFF, E], F32, kind="ExternalInput").ap()
    wo = nc.dram_tensor("wo", [HD, E], F32, kind="ExternalInput").ap()
    m_row = nc.dram_tensor("m_row", [1, TPC], F32, kind="ExternalInput").ap()
    r_row = nc.dram_tensor("r_row", [1, TPC], F32, kind="ExternalInput").ap()
    v_row = nc.dram_tensor("v_row", [1, TPC], F32, kind="ExternalInput").ap()
    ones1 = nc.dram_tensor("ones1", [1, 128], F32, kind="ExternalInput").ap()
    onesc = nc.dram_tensor("onesc", [128, 1], F32, kind="ExternalInput").ap()
    onescb = nc.dram_tensor("onescb", [128, 1], BF16, kind="ExternalInput").ap()

    s1d = nc.dram_tensor("s1d", [E, TPC], F32)
    o_x2 = nc.dram_tensor("x2", [E, TPC], F32, kind="ExternalOutput").ap()

    p = Prog()
    from contextlib import ExitStack
    with ExitStack() as st:
        En = st.enter_context
        sb_st = En(nc.sbuf_tensor("sb_st", [128, 1024], F32))
        sb_st2 = En(nc.sbuf_tensor("sb_st2", [128, 1024], F32))
        sb_stb = En(nc.sbuf_tensor("sb_stb", [128, 512], BF16))
        sb_wob = En(nc.sbuf_tensor("sb_wob", [128, 8 * 512], BF16))
        sb_wup = En(nc.sbuf_tensor("sb_wup", [128, 8 * DFF], BF16))
        sb_wdn = En(nc.sbuf_tensor("sb_wdn", [128, 32 * E], BF16))
        sb_h1 = En(nc.sbuf_tensor("sb_h1", [128, 32 * 256], BF16))
        sb_t2r = En(nc.sbuf_tensor("sb_t2r", [1, 512], F32))
        sb_sc2 = En(nc.sbuf_tensor("sb_sc2", [1, TPC], F32))
        sb_nm2 = En(nc.sbuf_tensor("sb_nm2", [1, TPC], BF16))
        sb_cs = En(nc.sbuf_tensor("sb_cs", [1, DFF], BF16))
        sb_al = En(nc.sbuf_tensor("sb_al", [128, 1], F32))
        sb_t1 = En(nc.sbuf_tensor("sb_t1", [1, 512], F32))
        sb_g1 = En(nc.sbuf_tensor("sb_g1", [128, 2 * 256], F32))
        sb_o = En(nc.sbuf_tensor("sb_o", [128, 512], F32))
        sb_o1 = En(nc.sbuf_tensor("sb_o1", [1, 128], F32))
        sb_oc = En(nc.sbuf_tensor("sb_oc", [128, 1], F32))
        sb_ocb = En(nc.sbuf_tensor("sb_ocb", [128, 1], BF16))
        sb_xr = En(nc.sbuf_tensor("sb_xr", [128, 8 * 512], BF16))
        ps_o = En(nc.psum_tensor("ps_o", [128, 512], F32))
        ps_r = En(nc.psum_tensor("ps_r", [1, 512], F32))
        ps_b = En(nc.psum_tensor("ps_b", [128, 512], F32))
        ps_q = [En(nc.psum_tensor(f"ps_q{i}", [128, 512], F32)) for i in range(4)]
        ps_o2 = En(nc.psum_tensor("ps_o2", [128, 512], F32))

        for f in [lambda h: h.dma_start(out=sb_o1[:], in_=ones1),
                  lambda h: h.dma_start(out=sb_oc[:], in_=onesc),
                  lambda h: h.dma_start(out=sb_ocb[:], in_=onescb)]:
            p.dma("sync", f)
        prepld = p.cnt["sync"]
        for e in ("scalar", "vector", "tensor"):
            p.wait(e, "sync", prepld)

        for og in range(2):
            for hc in range(8):
                p.wait("sync", "vector", p.cnt["vector"])
                wld = p.dma("sync", lambda h, og=og, hc=hc: h.dma_start(
                    out=sb_st[:, 0:512], in_=wo[hc * 128:(hc + 1) * 128,
                                               og * 512:(og + 1) * 512]))
                p.wait("vector", "sync", wld)
                wcv = p.op("vector", lambda h, hc=hc: h.tensor_copy(
                    sb_wob[:, hc * 512:(hc + 1) * 512], sb_st[:, 0:512]))
            for nt4 in range(4):
                for hc in range(8):
                    p.wait("sync", "vector", p.cnt["vector"])
                    yld = p.dma("sync", lambda h, hc=hc, nt4=nt4: h.dma_start(
                        out=sb_st2[:, 0:512], in_=yat_fm[hc * 128:(hc + 1) * 128,
                                                         nt4 * 512:(nt4 + 1) * 512]))
                    p.wait("vector", "sync", yld)
                    p.wait("vector", "tensor", p.cnt["tensor"])
                    ycv = p.op("vector", lambda h: h.tensor_copy(sb_stb[:], sb_st2[:, 0:512]))
                    p.wait("tensor", "vector", ycv)
                    for oi in range(4):
                        p.mm(lambda h, hc=hc, oi=oi: h.matmul(
                            ps_q[oi][:], sb_wob[:, hc * 512 + oi * 128: hc * 512 + (oi + 1) * 128],
                            sb_stb[:], start=(hc == 0), stop=(hc == 7),
                            skip_group_check=True))
                mmv = p.cnt["tensor"]
                p.wait("vector", "tensor", mmv)
                for oi in range(4):
                    ev = p.op("vector", lambda h, oi=oi: h.tensor_copy(sb_o[:], ps_q[oi][:]))
                    p.wait("gpsimd", "vector", ev)
                    p.dma("gpsimd", lambda h, og=og, oi=oi, nt4=nt4: h.dma_start(
                        out=s1d.ap()[(og * 4 + oi) * 128:(og * 4 + oi + 1) * 128,
                                     nt4 * 512:(nt4 + 1) * 512],
                        in_=sb_o[:]))
                    p.wait("vector", "gpsimd", p.cnt["gpsimd"])
                p.wait("tensor", "gpsimd", p.cnt["gpsimd"])

        def ternarize(wT_ap, DIN, DOUT, dest):
            nch = DIN // 128
            nsub = DOUT // 1024
            for dc in range(nch):
                for sub in range(nsub):
                    p.wait("sync", "scalar", p.cnt["scalar"])
                    p.wait("sync", "tensor", p.cnt["tensor"])
                    p.wait("sync", "vector", p.cnt["vector"])
                    ld = p.dma("sync", lambda h, dc=dc, sub=sub: h.dma_start(
                        out=sb_st[:], in_=wT_ap[dc * 128:(dc + 1) * 128,
                                               sub * 1024:(sub + 1) * 1024]))
                    p.wait("scalar", "sync", ld)
                    sv = p.op("scalar", lambda h: h.activation(
                        out=sb_st2[:, 0:512], in_=sb_st[:, 0:512], func=AF.Abs))
                    sv = p.op("scalar", lambda h: h.activation(
                        out=sb_st2[:, 512:1024], in_=sb_st[:, 512:1024], func=AF.Abs))
                    p.wait("tensor", "scalar", sv)
                    for nn in range(2):
                        p.mm(lambda h, nn=nn, dc=dc, sub=sub: h.matmul(
                            ps_r[:], sb_oc[:], sb_st2[:, nn * 512:(nn + 1) * 512],
                            start=(dc == 0 and sub == 0 and nn == 0),
                            stop=(dc == nch - 1 and sub == nsub - 1 and nn == 1),
                            skip_group_check=True))
            mmv = p.cnt["tensor"]
            p.wait("vector", "tensor", mmv)
            p.op("vector", lambda h: h.reduce_sum(sb_al[0:1, :], ps_r[:], axis=AX.X))
            av = p.op("vector", lambda h, DIN=DIN, DOUT=DOUT: h.tensor_scalar_mul(
                sb_al[0:1, :], sb_al[0:1, :], 0.5 / (DIN * DOUT)))
            p.wait("tensor", "vector", av)
            mmv = p.mm(lambda h: h.matmul(
                ps_b[:, 0:1], sb_o1[:], sb_al[0:1, :], start=True, stop=True))
            p.wait("vector", "tensor", mmv)
            p.op("vector", lambda h: h.tensor_copy(sb_al[:], ps_b[:, 0:1]))
            for dc in range(nch):
                for sub in range(nsub):
                    p.wait("sync", "vector", p.cnt["vector"])
                    p.wait("sync", "scalar", p.cnt["scalar"])
                    ld = p.dma("sync", lambda h, dc=dc, sub=sub: h.dma_start(
                        out=sb_st[:], in_=wT_ap[dc * 128:(dc + 1) * 128,
                                               sub * 1024:(sub + 1) * 1024]))
                    p.wait("scalar", "sync", ld)
                    p.op("scalar", lambda h: h.activation(
                        out=sb_st2[:], in_=sb_st[:], func=AF.Abs))
                    sv = p.op("scalar", lambda h: h.activation(
                        out=sb_st[:], in_=sb_st[:], func=AF.Sign))
                    p.wait("vector", "scalar", sv)
                    p.op("vector", lambda h: h.tensor_single_scalar(
                        out=sb_st2[:], in_=sb_st2[:], scalar=sb_al[:], op=ALU.is_gt))
                    p.op("vector", lambda h, dc=dc, sub=sub, dest=dest, DOUT=DOUT: h.tensor_tensor(
                        out=dest[:, dc * DOUT + sub * 1024: dc * DOUT + (sub + 1) * 1024],
                        in0=sb_st[:], in1=sb_st2[:], op=ALU.mult))

        ternarize(wupT, E, DFF, sb_wup)
        ternarize(wdnT, DFF, E, sb_wdn)
        wready = p.cnt["vector"]

        p.wait("tensor", "vector", wready)
        for ft in range(DFF // 512):
            for dc in range(8):
                p.mm(lambda h, ft=ft, dc=dc: h.matmul(
                    ps_r[:], sb_ocb[:], sb_wup[:, dc * DFF + ft * 512: dc * DFF + (ft + 1) * 512],
                    start=(dc == 0), stop=(dc == 7)))
            mmv = p.cnt["tensor"]
            p.wait("vector", "tensor", mmv)
            ev = p.op("vector", lambda h, ft=ft: h.tensor_copy(
                sb_cs[:, ft * 512:(ft + 1) * 512], ps_r[:]))
            p.wait("tensor", "vector", ev)

        for ch in range(TPC // 512):
            p.wait("sync", "vector", p.cnt["vector"])
            vld = p.dma("sync", lambda h, ch=ch: h.dma_start(
                out=sb_t1[:], in_=v_row[0:1, ch * 512:(ch + 1) * 512]))
            p.wait("vector", "sync", vld)
            p.op("vector", lambda h, ch=ch: h.tensor_scalar_add(
                sb_sc2[:, ch * 512:(ch + 1) * 512], sb_t1[:], 1e-5))
            p.op("vector", lambda h, ch=ch: h.reciprocal(
                sb_sc2[:, ch * 512:(ch + 1) * 512],
                sb_sc2[:, ch * 512:(ch + 1) * 512]))
            dv = p.op("vector", lambda h, ch=ch: h.tensor_tensor(
                out=sb_sc2[:, ch * 512:(ch + 1) * 512], in0=sb_t1[:],
                in1=sb_sc2[:, ch * 512:(ch + 1) * 512], op=ALU.mult))
            p.wait("scalar", "vector", dv)
            sv = p.op("scalar", lambda h, ch=ch: h.activation(
                out=sb_sc2[:, ch * 512:(ch + 1) * 512],
                in_=sb_sc2[:, ch * 512:(ch + 1) * 512], func=AF.Sqrt))
            p.wait("vector", "scalar", sv)
            p.op("vector", lambda h, ch=ch: h.tensor_scalar_add(
                sb_sc2[:, ch * 512:(ch + 1) * 512],
                sb_sc2[:, ch * 512:(ch + 1) * 512], 1e-8))
            p.op("vector", lambda h, ch=ch: h.reciprocal(
                sb_sc2[:, ch * 512:(ch + 1) * 512],
                sb_sc2[:, ch * 512:(ch + 1) * 512]))
            p.wait("sync", "vector", p.cnt["vector"])
            rld = p.dma("sync", lambda h, ch=ch: h.dma_start(
                out=sb_t1[:], in_=r_row[0:1, ch * 512:(ch + 1) * 512]))
            p.wait("vector", "sync", rld)
            p.op("vector", lambda h, ch=ch: h.tensor_tensor(
                out=sb_sc2[:, ch * 512:(ch + 1) * 512], in0=sb_t1[:],
                in1=sb_sc2[:, ch * 512:(ch + 1) * 512], op=ALU.mult))
            p.wait("sync", "vector", p.cnt["vector"])
            mld = p.dma("sync", lambda h, ch=ch: h.dma_start(
                out=sb_t2r[:], in_=m_row[0:1, ch * 512:(ch + 1) * 512]))
            p.wait("vector", "sync", mld)
            p.op("vector", lambda h, ch=ch: h.tensor_tensor(
                out=sb_t1[:], in0=sb_t2r[:],
                in1=sb_sc2[:, ch * 512:(ch + 1) * 512], op=ALU.mult))
            nv = p.op("vector", lambda h, ch=ch: h.tensor_scalar_mul(
                sb_nm2[:, ch * 512:(ch + 1) * 512], sb_t1[:], -1.0))
            p.wait("sync", "vector", nv)

        for nt in range(TPC // 256):
            p.wait("tensor", "vector", p.cnt["vector"])
            mmv = p.mm(lambda h, nt=nt: h.matmul(
                ps_b[:, 0:256], sb_o1[:], sb_sc2[:, nt * 256:(nt + 1) * 256],
                start=True, stop=True))
            p.wait("vector", "tensor", mmv)
            for dc in range(8):
                xld = p.dma("sync", lambda h, dc=dc, nt=nt: h.dma_start(
                    out=sb_st[:, 0:256], in_=x1_fm[dc * 128:(dc + 1) * 128,
                                                   nt * 256:(nt + 1) * 256]))
                p.wait("vector", "sync", xld)
                xv = p.op("vector", lambda h, dc=dc: h.tensor_tensor(
                    out=sb_xr[:, dc * 512: dc * 512 + 256],
                    in0=sb_st[:, 0:256], in1=ps_b[:, 0:256], op=ALU.mult))
                p.wait("sync", "vector", xv)
            xr_ready = p.cnt["vector"]
            scd = [0, 0]
            for fc in range(32):
                par = fc % 2
                po = ps_o if par == 0 else ps_o2
                p.wait("tensor", "vector", xr_ready)
                p.wait("tensor", "scalar", scd[par])
                fs = []
                for dc in range(8):
                    def mmf(h, dc=dc, fc=fc, nt=nt, po=po):
                        return h.matmul(
                            po[:, 0:256],
                            sb_wup[:, dc * DFF + fc * 128: dc * DFF + (fc + 1) * 128],
                            sb_xr[:, dc * 512: dc * 512 + 256],
                            start=(dc == 0), stop=False)
                    fs.append(mmf)
                def mmf_last(h, fc=fc, nt=nt, po=po):
                    return h.matmul(
                        po[:, 0:256], sb_cs[:, fc * 128:(fc + 1) * 128],
                        sb_nm2[:, nt * 256:(nt + 1) * 256], start=False, stop=True)
                fs.append(mmf_last)
                mmv = p.mm_group(fs)
                p.wait("scalar", "tensor", mmv)
                scd[par] = p.op("scalar", lambda h, fc=fc, po=po: h.activation(
                    out=sb_h1[:, fc * 256:(fc + 1) * 256], in_=po[:, 0:256], func=AF.Gelu))
            p.wait("tensor", "scalar", p.cnt["scalar"])
            for fc in range(32):
                p.wait("scalar", "tensor", p.cnt["tensor"])
                sv = p.op("scalar", lambda h, fc=fc: h.activation(
                    out=sb_g1[:, 256:512], in_=sb_h1[:, fc * 256:(fc + 1) * 256], func=AF.Square))
                p.wait("tensor", "scalar", sv)
                p.mm(lambda h, fc=fc: h.matmul(
                    ps_r[:, 0:256], sb_oc[:], sb_g1[:, 256:512], start=(fc == 0), stop=(fc == 31),
                    skip_group_check=True))
            mmv = p.cnt["tensor"]
            p.wait("scalar", "tensor", mmv)
            sv = p.op("scalar", lambda h: h.activation(
                out=sb_t1[:, 0:256], in_=ps_r[:, 0:256], func=AF.Sqrt, scale=1.0 / DFF))
            p.wait("vector", "scalar", sv)
            p.op("vector", lambda h: h.tensor_scalar_add(sb_t1[:, 0:256], sb_t1[:, 0:256], 1e-8))
            rv = p.op("vector", lambda h: h.reciprocal(sb_t1[:, 0:256], sb_t1[:, 0:256]))
            p.wait("tensor", "vector", rv)
            mmv = p.mm(lambda h: h.matmul(
                ps_b[:, 0:256], sb_o1[:], sb_t1[:, 0:256], start=True, stop=True))
            p.wait("vector", "tensor", mmv)
            for fc in range(32):
                p.op("vector", lambda h, fc=fc: h.tensor_tensor(
                    out=sb_h1[:, fc * 256:(fc + 1) * 256],
                    in0=sb_h1[:, fc * 256:(fc + 1) * 256], in1=ps_b[:, 0:256], op=ALU.mult))
            h1ready = p.cnt["vector"]
            dnd = [0, 0]
            for dc in range(8):
                par = dc % 2
                po = ps_o if par == 0 else ps_o2
                p.wait("tensor", "vector", max(h1ready, p.cnt["vector"]))
                p.wait("tensor", "scalar", dnd[par])
                fs = []
                for fc in range(32):
                    def mmf(h, dc=dc, fc=fc, po=po):
                        return h.matmul(
                            po[:, 0:256], sb_wdn[:, fc * E + dc * 128: fc * E + (dc + 1) * 128],
                            sb_h1[:, fc * 256:(fc + 1) * 256],
                            start=(fc == 0), stop=(fc == 31))
                    fs.append(mmf)
                mmv = p.mm_group(fs)
                p.wait("scalar", "tensor", mmv)
                dnd[par] = p.op("scalar", lambda h, par=par, po=po: h.activation(
                    out=sb_g1[:, par * 256:par * 256 + 256], in_=po[:, 0:256], func=AF.Silu))
                sld = p.dma("sync", lambda h, dc=dc, nt=nt: h.dma_start(
                    out=sb_st[:, 0:256], in_=s1d.ap()[dc * 128:(dc + 1) * 128,
                                                      nt * 256:(nt + 1) * 256]))
                xld = p.dma("sync", lambda h, dc=dc, nt=nt: h.dma_start(
                    out=sb_st[:, 512:768], in_=x1_fm[dc * 128:(dc + 1) * 128,
                                                     nt * 256:(nt + 1) * 256]))
                p.wait("vector", "sync", xld)
                p.wait("vector", "scalar", dnd[par])
                p.op("vector", lambda h, par=par: h.tensor_tensor(
                    out=sb_o[:, 0:256], in0=sb_st[:, 0:256],
                    in1=sb_g1[:, par * 256:par * 256 + 256], op=ALU.mult))
                ov = p.op("vector", lambda h: h.tensor_tensor(
                    out=sb_o[:, 0:256], in0=sb_o[:, 0:256], in1=sb_st[:, 512:768], op=ALU.add))
                p.wait("gpsimd", "vector", ov)
                p.dma("gpsimd", lambda h, dc=dc, nt=nt: h.dma_start(
                    out=o_x2[dc * 128:(dc + 1) * 128, nt * 256:(nt + 1) * 256],
                    in_=sb_o[:, 0:256]))
                p.wait("vector", "gpsimd", p.cnt["gpsimd"])
                p.wait("sync", "vector", p.cnt["vector"])
        p.barrier_out("gpsimd")
        p.build(nc)
    return nc


def host_L8(inputs, cache, run):
    wupT = np.ascontiguousarray(np.asarray(inputs["L1_ffn_w_up"], np.float32).T)
    wdnT = np.ascontiguousarray(np.asarray(inputs["L1_ffn_w_down"], np.float32).T)
    wo = np.ascontiguousarray(np.asarray(inputs["L1_attn_w_out"], np.float32))
    ones1 = np.ones((1, 128), np.float32)
    onesc = np.ones((128, 1), np.float32)
    onescb = onesc.astype(_bf16())
    nc = build_L8()
    in_maps = []
    for c in range(NC):
        tok = slice(c * TPC, (c + 1) * TPC)
        in_maps.append(dict(
            x1_fm=np.ascontiguousarray(cache["x1_all"][:, tok]),
            yat_fm=np.ascontiguousarray(cache["yatt_all"][:, tok]),
            wupT=wupT, wdnT=wdnT, wo=wo,
            m_row=cache["m1_all"][tok].reshape(1, TPC),
            r_row=cache["r1_all"][tok].reshape(1, TPC),
            v_row=cache["v1_all"][tok].reshape(1, TPC),
            ones1=ones1, onesc=onesc, onescb=onescb))
    res = run(nc, in_maps)
    cache["x2_all"] = np.concatenate([np.asarray(res[c]["x2"]) for c in range(NC)], 1)


def host_L1(inputs, cache, run):
    x = np.asarray(inputs["L0_attn_x"], np.float32).reshape(S, E)
    wzT = np.ascontiguousarray(np.asarray(inputs["L0_attn_W_z"], np.float32).T)
    wpg = np.concatenate([np.asarray(inputs["L0_attn_W_params"], np.float32),
                          np.asarray(inputs["L0_ffn_gate_w"], np.float32)], 0)
    wpgT = np.ascontiguousarray(wpg.T)
    iden = np.eye(128, dtype=np.float32).astype(_bf16())
    nc = build_L1()
    in_maps = [dict(x_tok=np.ascontiguousarray(x[c * TPC:(c + 1) * TPC]),
                    wzT=wzT, wpgT=wpgT, iden=iden) for c in range(NC)]
    res = run(nc, in_maps)
    cache["z_all"] = np.concatenate([np.asarray(res[c]["z"]) for c in range(NC)], 1)
    cache["pg_all"] = np.concatenate([np.asarray(res[c]["pg"]) for c in range(NC)], 1)
    cache["m_all"] = np.concatenate(
        [np.asarray(res[c]["m"]).T.reshape(-1) for c in range(NC)]).astype(np.float32)
    cache["r_all"] = np.concatenate(
        [np.asarray(res[c]["r"]).T.reshape(-1) for c in range(NC)]).astype(np.float32)
    cache["xn_all"] = np.concatenate([np.asarray(res[c]["xn"]) for c in range(NC)], 0)


def _run_device(nc, in_maps):
    res = run_bass_kernel_spmd(nc, in_maps, core_ids=list(range(len(in_maps))))
    return res.results


def kernel(**inputs):
    cache = {}
    host_L1(inputs, cache, _run_device)
    host_L2(inputs, cache, _run_device)
    host_L3(inputs, cache, _run_device)
    host_L4(inputs, cache, _run_device)
    host_L5(inputs, cache, _run_device)
    host_L6(inputs, cache, _run_device)
    host_L7(inputs, cache, _run_device)
    host_L8(inputs, cache, _run_device)
    x2 = cache["x2_all"]
    return np.ascontiguousarray(x2.T).reshape(B, L, E).astype(np.float32)
